# version 29
# baseline (speedup 1.0000x reference)
import contextlib
import numpy as np
import concourse.bass as bass
import concourse.mybir as mybir
from concourse.bass_utils import run_bass_kernel_spmd
from concourse.alu_op_type import AluOpType as ALU

F32 = mybir.dt.float32
F32R = mybir.dt.float32r
BF16 = mybir.dt.bfloat16


def r32(ap):
    return ap.bitcast(mybir.dt.float32r)
AF = mybir.ActivationFunctionType
AX = mybir.AxisListType

D = 4096
NCH = 32
TOK = 2048
TAU = 512
NTAU = TOK // TAU
KVD = 1024
NKVC = 8
NH = 32
SEQ = 4096
NKC = SEQ // 128
EPS = 1e-5
INV_SQRT_HD = float(1.0 / np.sqrt(128.0))

STAGE = "full"
DBG = False
CACHE_W = True

ENGINES = ("tensor", "vector", "scalar", "gpsimd", "sync")


class _Sem:
    __slots__ = ("h", "n")

    def __init__(self, h):
        self.h = h
        self.n = 0


class Ctx:
    def __init__(self, nc, active, sems):
        self.nc = nc
        self.active = active
        self.sems = sems

    AUTO = {"vector": "dve", "scalar": "act"}

    def op(self, eng, fn, inc=None, inc_n=1, chain=True):
        auto = chain and eng in self.AUTO
        if auto:
            s = self.AUTO[eng]
            if inc is None:
                inc, inc_n = s, 1
            assert inc == s and inc_n == 1, (eng, inc)
            self.wait(eng, s, self.sems[s].n)
        if eng == self.active:
            inst = fn(getattr(self.nc, eng))
            if inc is not None:
                inst.then_inc(self.sems[inc].h, inc_n)
        if inc is not None:
            self.sems[inc].n += inc_n

    def dma(self, eng, out, in_, inc, **kw):
        self.op(eng, lambda E: E.dma_start(out=out, in_=in_, **kw),
                inc=inc, inc_n=16, chain=False)

    def wait(self, eng, sem, val=None):
        v = self.sems[sem].n if val is None else val
        if eng == self.active:
            getattr(self.nc, eng).wait_ge(self.sems[sem].h, v)

    def val(self, sem):
        return self.sems[sem].n

    def barrier(self):
        for e in ENGINES:
            if e in self.AUTO:
                self.wait(e, self.AUTO[e], self.sems[self.AUTO[e]].n)
            self.op(e, lambda E: E.nop(), inc="bar", inc_n=1, chain=False)
        tgt = self.val("bar")
        for e in ENGINES:
            self.wait(e, "bar", tgt)



def wait_xc(ctx, eng, xc):
    if isinstance(xc, tuple):
        ctx.wait(eng, xc[0], xc[1])
    else:
        ctx.wait(eng, "dve", xc)


def emit_ln(ctx, T, wait_slab):
    emit_ln_stats(ctx, T, wait_slab)
    emit_ln_norm(ctx, T)


def emit_ln_stats(ctx, T, wait_slab):
    L = T.led
    wfn = wait_slab if callable(wait_slab) else (lambda c: wait_slab)
    for c in range(NCH):
        xb = T.xbf_f[c % 2]
        sq = T.sqb_f[c % 2]
        ws = wfn(c)
        ctx.wait("scalar", ws[0], ws[1])
        ctx.wait("scalar", "pe", L["sq_free"][c % 2])
        ctx.wait("vector", ws[0], ws[1])
        ctx.wait("vector", "pe", L["xb_free"][c % 2])
        if c == 0:
            ctx.wait("tensor", "dve", L["pm_free"])
            ctx.wait("tensor", "act")
        ctx.op("vector", lambda E, c=c, xb=xb: E.tensor_copy(
            xb[:, :], T.slab[:, c * TAU:(c + 1) * TAU]), inc="dve")
        ctx.op("scalar", lambda E, c=c, sq=sq: E.activation(
            out=sq[:, :], in_=T.slab[:, c * TAU:(c + 1) * TAU], func=AF.Square),
            inc="act")
        ctx.wait("tensor", "dve")
        ctx.op("tensor", lambda E, c=c, xb=xb: E.matmul(
            T.pm[:, :], T.invdb[:, :], xb[:, :],
            start=(c == 0), stop=(c == NCH - 1)))
        ctx.wait("tensor", "act")
        ctx.op("tensor", lambda E, c=c, sq=sq: E.matmul(
            T.ps[:, :], T.invdb[:, :], sq[:, :],
            start=(c == 0), stop=(c == NCH - 1)), inc="pe")
        L["sq_free"][c % 2] = ctx.val("pe")
        L["xb_free"][c % 2] = ctx.val("pe")
    ctx.wait("vector", "pe")
    ctx.op("vector", lambda E: E.tensor_copy(T.m_row[:, :], T.pm[:, :]))
    ctx.op("vector", lambda E: E.tensor_copy(T.v_row[:, :], T.ps[:, :]))
    ctx.op("vector", lambda E: E.scalar_tensor_tensor(
        out=T.mr_row[:, :], in0=T.m_row[:, :], scalar=1.0, in1=T.m_row[:, :],
        op0=ALU.mult, op1=ALU.mult))
    ctx.op("vector", lambda E: E.scalar_tensor_tensor(
        out=T.v_row[:, :], in0=T.v_row[:, :], scalar=-1.0, in1=T.mr_row[:, :],
        op0=ALU.mult, op1=ALU.subtract))
    ctx.op("vector", lambda E: E.tensor_scalar(
        out=T.v_row[:, :], in0=T.v_row[:, :], scalar1=EPS, scalar2=None,
        op0=ALU.add), inc="dve")
    L["pm_free"] = ctx.val("dve")
    ctx.wait("scalar", "dve")
    ctx.op("scalar", lambda E: E.activation(
        out=T.mr_row[:, :], in_=T.v_row[:, :], func=AF.Ln))
    ctx.op("scalar", lambda E: E.activation(
        out=T.r_row[:, :], in_=T.mr_row[:, :], func=AF.Exp, scale=-0.5),
        inc="act")
    ctx.wait("vector", "act")
    ctx.op("vector", lambda E: E.tensor_tensor(
        T.mr_row[:, :], T.m_row[:, :], T.r_row[:, :], ALU.mult), inc="dve")
    ctx.wait("tensor", "dve")
    ctx.op("tensor", lambda E: E.matmul(
        T.pmisc[:, :], T.ones_row[:, :], T.r_row[:, :],
        start=True, stop=True), inc="pe")
    ctx.wait("vector", "pe")
    ctx.op("vector", lambda E: E.tensor_copy(T.bc_r[:, :], T.pmisc[:, :]),
           inc="dve")
    ctx.wait("tensor", "dve")
    ctx.op("tensor", lambda E: E.matmul(
        T.pmisc[:, :], T.ones_row[:, :], T.mr_row[:, :],
        start=True, stop=True), inc="pe")
    ctx.wait("vector", "pe")
    ctx.op("vector", lambda E: E.tensor_copy(T.bc_mr[:, :], T.pmisc[:, :]),
           inc="dve")
    L["bc_ready"] = ctx.val("dve")


def _rep_ap(t, n):
    return bass.AP(t, 0, [[TAU, 128], [0, n], [1, TAU]])


def emit_ln_norm(ctx, T):
    L = T.led
    gsp = getattr(T, "gp_norm", 8)
    bc_v = L["bc_ready"]
    ctx.wait("gpsimd", "dve", bc_v)
    ctx.wait("gpsimd", "pe", ctx.val("pe"))
    NB = 2
    for c0 in range(0, NCH, NB):
        gp = c0 >= NCH - gsp
        xnb = T.xn[:, c0 * TAU:(c0 + NB) * TAU].rearrange(
            "p (r f) -> p r f", r=NB)
        slb = T.slab[:, c0 * TAU:(c0 + NB) * TAU].rearrange(
            "p (r f) -> p r f", r=NB)
        if gp:
            ctx.op("gpsimd", lambda E, xnb=xnb, slb=slb: E.tensor_tensor(
                xnb, slb, _rep_ap(T.bc_r, NB), ALU.mult))
            ctx.op("gpsimd", lambda E, xnb=xnb: E.tensor_tensor(
                xnb, xnb, _rep_ap(T.bc_mr, NB), ALU.add), inc="gp")
            v = ("gp", ctx.val("gp"))
        else:
            btb = T.bt[:, 0:NB * TAU].rearrange("p (r f) -> p r f", r=NB)
            ctx.op("vector", lambda E, btb=btb, slb=slb: E.tensor_tensor(
                btb, slb, _rep_ap(T.bc_r, NB), ALU.mult))
            ctx.op("vector", lambda E, xnb=xnb, btb=btb: E.tensor_tensor(
                xnb, btb, _rep_ap(T.bc_mr, NB), ALU.add), inc="dve")
            v = ("dve", ctx.val("dve"))
        for c in range(c0, c0 + NB):
            L["xn_chunk"][c] = v
    L["xn_ready"] = ctx.val("dve")
    L["xn_ready_gp"] = ctx.val("gp")


def emit_gate_top2(ctx, T):
    L = T.led
    ctx.wait("tensor", "dve", max(L["bc_ready"], L["pglog_free"]))
    for i in range(NCH):
        ctx.op("tensor", lambda E, i=i: E.matmul(
            T.pglog[:, :], T.wgf_sb[:, i * 8:(i + 1) * 8],
            T.slab[:, i * TAU:(i + 1) * TAU],
            start=(i == 0), stop=(i == NCH - 1)), inc="pe")
    ctx.wait("vector", "pe")
    ctx.op("vector", lambda E: E.tensor_tensor(
        T.wfT[:, :], T.pglog[:, :], T.bc_r[0:8, :], ALU.mult))
    ctx.op("vector", lambda E: E.scalar_tensor_tensor(
        out=T.glog_sb[:, :], in0=T.bc_mr[0:8, :], scalar=T.gws_col[:, 0:1],
        in1=T.wfT[:, :], op0=ALU.mult, op1=ALU.add), inc="dve")
    L["pglog_free"] = ctx.val("dve")
    for tt in range(TAU // 128):
        ctx.wait("tensor", "dve")
        ctx.op("tensor", lambda E, tt=tt: E.transpose(
            T.ptr[:, 0:8], T.glog_sb[0:8, tt * 128:(tt + 1) * 128],
            T.ident_f[0:8, 0:8]), inc="pe")
        ctx.wait("vector", "pe")
        ctx.op("vector", lambda E, tt=tt: E.tensor_copy(
            T.logtm[:, tt * 8:(tt + 1) * 8], T.ptr[:, 0:8]), inc="dve")
    lg3 = T.logtm[:, :].rearrange("p (r f) -> p r f", r=4)
    eq3 = T.eq1[:, :].rearrange("p (r f) -> p r f", r=4)
    l23 = T.l2[:, :].rearrange("p (r f) -> p r f", r=4)
    wf3 = T.wf[:, :].rearrange("p (r f) -> p r f", r=4)

    def b3(m):
        return bass.AP(m, 0, [[4, 128], [1, 4], [0, 8]])
    for tt in range(TAU // 128):
        ctx.op("vector", lambda E, tt=tt: E.reduce_max(
            T.max1[:, tt:tt + 1], T.logtm[:, tt * 8:(tt + 1) * 8], axis=AX.X))
    ctx.op("vector", lambda E: E.tensor_tensor(
        eq3, lg3, b3(T.max1), ALU.is_equal))
    ctx.op("vector", lambda E: E.scalar_tensor_tensor(
        out=T.l2[:, :], in0=T.eq1[:, :], scalar=-1e30, in1=T.logtm[:, :],
        op0=ALU.mult, op1=ALU.add))
    for tt in range(TAU // 128):
        ctx.op("vector", lambda E, tt=tt: E.reduce_max(
            T.max2[:, tt:tt + 1], T.l2[:, tt * 8:(tt + 1) * 8], axis=AX.X))
    ctx.op("vector", lambda E: E.tensor_tensor(
        T.zrow[:, :], T.max2[:, :], T.max1[:, :], ALU.subtract), inc="dve")
    ctx.wait("scalar", "dve")
    ctx.op("scalar", lambda E: E.activation(
        out=T.e2[:, :], in_=T.zrow[:, :], func=AF.Exp, scale=1.0), inc="act")
    ctx.wait("vector", "act")
    ctx.op("vector", lambda E: E.tensor_scalar(
        out=T.zrow[:, :], in0=T.e2[:, :], scalar1=1.0, scalar2=None,
        op0=ALU.add))
    ctx.op("vector", lambda E: E.reciprocal(T.w1[:, :], T.zrow[:, :]))
    ctx.op("vector", lambda E: E.tensor_tensor(
        T.w2[:, :], T.e2[:, :], T.w1[:, :], ALU.mult))
    ctx.op("vector", lambda E: E.tensor_tensor(
        wf3, eq3, b3(T.w1), ALU.mult))
    ctx.op("vector", lambda E: E.tensor_tensor(
        eq3, l23, b3(T.max2), ALU.is_equal))
    ctx.op("vector", lambda E: E.tensor_tensor(
        l23, eq3, b3(T.w2), ALU.mult))
    ctx.op("vector", lambda E: E.tensor_tensor(
        T.wf[:, :], T.wf[:, :], T.l2[:, :], ALU.add), inc="dve")
    if getattr(T, "dump", None) is not None and T.cur_tau == 0:
        ctx.wait("gpsimd", "dve")
        ctx.dma("gpsimd", T.dump["wf"][:, :], T.wf[:, :], inc="gd")
        ctx.wait("vector", "gd", ctx.val("gd"))
        ctx.wait("tensor", "gd", ctx.val("gd"))
    for tt in range(TAU // 128):
        ctx.wait("tensor", "dve")
        ctx.op("tensor", lambda E, tt=tt: E.transpose(
            T.ptr[0:8, 0:128], T.wf[:, tt * 8:(tt + 1) * 8],
            T.ident_f[:, :]), inc="pe")
        ctx.wait("vector", "pe")
        ctx.op("vector", lambda E, tt=tt: E.tensor_copy(
            T.wfT[0:8, tt * 128:(tt + 1) * 128], T.ptr[0:8, 0:128]), inc="dve")
    ctx.wait("tensor", "dve")
    ctx.op("tensor", lambda E: E.matmul(
        T.pmisc[:, :], T.pat2[:, :], T.wfT[:, :],
        start=True, stop=True), inc="pe")
    if getattr(T, "dump", None) is not None and T.cur_tau == 0:
        ctx.wait("gpsimd", "dve")
        ctx.wait("gpsimd", "pe")
        ctx.dma("gpsimd", T.dump["glog"][:, :], T.glog_sb[:, :], inc="gd")
        ctx.dma("gpsimd", T.dump["wfT"][:, :], T.wfT[:, :], inc="gd")
        ctx.wait("vector", "gd", ctx.val("gd"))
        ctx.wait("tensor", "gd", ctx.val("gd"))


def emit_low_sl(ctx, T):
    L = T.led
    ctx.wait("tensor", "dve", L["acc_free"][0])
    for i in range(NCH):
        wait_xc(ctx, "tensor", L["xn_chunk"][i])
        ctx.op("tensor", lambda E, i=i: E.matmul(
            T.pacc[0][:, :], T.wa_sb[:, i * 128:(i + 1) * 128],
            T.xn[:, i * TAU:(i + 1) * TAU],
            start=(i == 0), stop=(i == NCH - 1)), inc="pe")
    ctx.wait("vector", "pe")
    ctx.op("vector", lambda E: E.tensor_copy(T.low_sb[:, :], T.pacc[0][:, :]))
    ctx.op("vector", lambda E: E.tensor_tensor(
        T.sl_bf[:, :], T.low_sb[:, :], T.pmisc[:, :], ALU.mult), inc="dve")
    L["sl_ready"] = ctx.val("dve")
    L["acc_free"][0] = ctx.val("dve")
    if getattr(T, "dump", None) is not None and T.cur_tau == 0:
        ctx.wait("gpsimd", "dve")
        ctx.dma("gpsimd", T.dump["low"][:, :], T.low_sb[:, :], inc="gd")
        ctx.dma("gpsimd", T.dump["sl"][:, :], T.sl_bf[:, :], inc="gd")


def emit_wtile(ctx, T, wd, pwd, j, tau, kind, ring, fast=False):
    L = T.led
    wbf = T.wbf[ring]
    if not CACHE_W:
        tau = 0
    if tau == 0:
        if kind == "sign":
            wl = T.wload[ring]
            H2 = D // 2
            hv = []
            rv = []
            wlds = []
            for hf in range(2):
                sl_ = slice(hf * H2, (hf + 1) * H2)
                lq = "scalar" if (fast and hf == 1) else "sync"
                wf_ = L["wload_free"][ring]
                if isinstance(wf_, tuple) and wf_[0] == "pe":
                    ctx.wait(lq, "pe", wf_[1])
                elif isinstance(wf_, tuple):
                    ctx.wait(lq, "act", wf_[0])
                    ctx.wait(lq, "dve", wf_[1])
                else:
                    ctx.wait(lq, "pe", wf_)
                sem = "wl2" if lq == "scalar" else "wld"
                ctx.dma(lq, wl[:, sl_], wd[j][:, sl_], inc=sem)
                wlds.append((sem, ctx.val(sem)))
            for hf in range(2):
                sl_ = slice(hf * H2, (hf + 1) * H2)
                ctx.wait("scalar", wlds[hf][0], wlds[hf][1])
                if hf == 0:
                    ctx.wait("scalar", "pe", L["wbf_free"][ring])
                    ctx.wait("scalar", "gd", L["wbf_gd"][ring])
                ctx.op("scalar", lambda E, wl=wl, wbf=wbf, sl_=sl_: E.activation(
                    out=wbf[:, sl_], in_=wl[:, sl_], func=AF.Sign), inc="act")
                hv.append(ctx.val("act"))
                ctx.wait("vector", wlds[hf][0], wlds[hf][1])
                ctx.op("vector", lambda E, wl=wl, sl_=sl_, hf=hf: E.reduce_sum(
                    T.sh2[:, hf:hf + 1], wl[:, sl_], axis=AX.X,
                    apply_absolute_value=True), inc="dve")
                rv.append(ctx.val("dve"))
            wbq = "gpsimd" if fast else "scalar"
            ctx.wait(wbq, "act", hv[1])
            ctx.dma(wbq, pwd[j], wbf[:, :], inc="gd")
            L["wbf_gd"][ring] = ctx.val("gd")
            if T.wload_single:
                L["wload_free"][0] = L["wload_free"][1] = ("pe", ctx.val("pe"))
            else:
                L["wload_free"][ring] = (hv[1], rv[1])
            scol = T.scol_cur
            ctx.op("vector", lambda E, scol=scol: E.tensor_tensor(
                scol, T.sh2[:, 0:1], T.sh2[:, 1:2], ALU.add))
            ctx.op("vector", lambda E, scol=scol: E.tensor_scalar(
                out=scol, in0=scol, scalar1=1.0 / D, scalar2=None,
                op0=ALU.mult), inc="dve")
            return ("act", hv[0], hv[1])
        else:
            ctx.wait("gpsimd", "pe", L["wbf_free"][ring])
            ctx.wait("gpsimd", "gd", L["wbf_gd"][ring])
            ctx.dma("gpsimd", wbf[:, :], wd[j], inc="gd")
            gd_v = ctx.val("gd")
            ctx.wait("sync", "gd", gd_v)
            ctx.dma("sync", pwd[j], wbf[:, :], inc="gd")
            L["wbf_gd"][ring] = ctx.val("gd")
            return ("gd", gd_v)
    else:
        ctx.wait("sync", "pe", L["wbf_free"][ring])
        ctx.wait("sync", "gd", L["prep_gd"])
        ctx.wait("sync", "gd", L["wbf_gd"][ring])
        ctx.dma("sync", wbf[:, :], pwd[j], inc="wld")
        return ("wld", ctx.val("wld"))


def emit_ffn_base(ctx, T, tau):
    L = T.led
    for j in range(NCH):
        ring = j % 2
        w_ready = emit_wtile(ctx, T, T.wb_d, T.pwb_d, j, tau, "cast", ring)
        wbf = T.wbf[ring]
        acc = T.pacc[j % 2]
        ctx.wait("tensor", w_ready[0], w_ready[1])
        ctx.wait("tensor", "dve", L["acc_free"][j % 2])
        for i in range(NCH):
            if j == 0:
                wait_xc(ctx, "tensor", L["xn_chunk"][i])
            ctx.op("tensor", lambda E, i=i, wbf=wbf, acc=acc: E.matmul(
                acc[:, :], wbf[:, i * 128:(i + 1) * 128],
                T.xn[:, i * TAU:(i + 1) * TAU],
                start=(i == 0), stop=False))
        if j == 0:
            ctx.wait("tensor", "dve", L["sl_ready"])
        ctx.op("tensor", lambda E, j=j, acc=acc: E.matmul(
            acc[:, :], T.wbp_sb[:, j * 128:(j + 1) * 128], T.sl_bf[:, :],
            start=False, stop=True), inc="pe")
        L["wbf_free"][ring] = ctx.val("pe")
        ctx.wait("vector", "pe")
        ev = T.ev_f[j % 2]
        ctx.op("vector", lambda E, j=j, acc=acc, ev=ev: E.tensor_scalar(
            out=ev[:, :], in0=acc[:, :], scalar1=0.5,
            scalar2=T.hbb_sb[:, j:j + 1], op0=ALU.mult, op1=ALU.add))
        ctx.op("vector", lambda E, j=j, ev=ev: E.tensor_tensor(
            T.slab[:, j * TAU:(j + 1) * TAU], T.slab[:, j * TAU:(j + 1) * TAU],
            ev[:, :], ALU.add), inc="dve")
        L["acc_free"][j % 2] = ctx.val("dve")
        L["x1_chunk"][j] = ctx.val("dve")
    L["x1_ready"] = ctx.val("dve")


def emit_qkv(ctx, T, tau):
    L = T.led
    pend = []

    def flush_pend(ctx):
        for fn in pend:
            fn(ctx)
        del pend[:]

    def defer_norm(ring, act_v, maxrow):
        def norm(ctx, ring=ring, act_v=act_v, maxrow=maxrow):
            sqb = T.sqb_f[ring]
            ctx.wait("tensor", "act", act_v)
            ctx.wait("tensor", "dve", L["pm_free"])
            ctx.op("tensor", lambda E, sqb=sqb: E.matmul(
                T.pm[:, :], T.ones_cb[:, :], sqb[:, :], start=True, stop=True),
                inc="pe")
            L["sqb_free"][ring] = ctx.val("pe")
            ctx.wait("vector", "pe")
            ctx.op("vector", lambda E: E.reduce_max(
                T.t11[:, :], T.pm[:, :], axis=AX.X))
            ctx.op("vector", lambda E, maxrow=maxrow: E.tensor_tensor(
                maxrow[:, :], maxrow[:, :], T.t11[:, :], ALU.max), inc="dve")
            L["pm_free"] = ctx.val("dve")
        pend.append(norm)

    for j in range(NKVC):
        ring = j % 2
        T.scol_cur = T.sk_cols[:, j:j + 1]
        w_ready = emit_wtile(ctx, T, T.wk_d, T.pwk_d, j, tau, "sign", ring, fast=True)
        wbf = T.wbf[ring]
        acc = T.pacc[j % 2]
        ctx.wait("tensor", "dve", L["acc_free"][j % 2])
        for i in range(NCH):
            if i == 0:
                ctx.wait("tensor", w_ready[0], w_ready[1])
            elif i == NCH // 2 and len(w_ready) > 2:
                ctx.wait("tensor", w_ready[0], w_ready[2])
            if j == 0:
                wait_xc(ctx, "tensor", L["xn_chunk"][i])
            ctx.op("tensor", lambda E, i=i, wbf=wbf, acc=acc: E.matmul(
                acc[:, :], wbf[:, i * 128:(i + 1) * 128],
                T.xn[:, i * TAU:(i + 1) * TAU],
                start=(i == 0), stop=(i == NCH - 1)), inc="pe")
        L["wbf_free"][ring] = ctx.val("pe")
        flush_pend(ctx)
        ctx.wait("vector", "pe")
        kqf = L["kq_free"][j % 2]
        if isinstance(kqf, tuple):
            ctx.wait("vector", kqf[0], kqf[1])
        else:
            ctx.wait("vector", "gd", kqf)
        kq = T.kq_bf[j % 2]
        ctx.op("vector", lambda E, acc=acc, kq=kq: E.tensor_copy(
            kq[:, :], acc[:, :]), inc="dve")
        dve_v = ctx.val("dve")
        ctx.wait("gpsimd", "dve", dve_v)
        ctx.dma("gpsimd", T.kin_d[tau][j * 128:(j + 1) * 128, :], kq[:, :],
                inc="kv")
        L["kq_free"][j % 2] = ("kv", ctx.val("kv"))
        ctx.wait("scalar", "dve", dve_v)
        ctx.wait("scalar", "pe", L["sqb_free"][j % 2])
        sqb = T.sqb_f[j % 2]
        ctx.op("scalar", lambda E, acc=acc, sqb=sqb: E.activation(
            out=sqb[:, :], in_=acc[:, :], func=AF.Square), inc="act")
        L["acc_free"][j % 2] = ctx.val("act")
        defer_norm(j % 2, ctx.val("act"), T.maxk2)
    if tau == 0:
        ctx.wait("vector", "dve")
        for hk in range(NKVC):
            ctx.op("vector", lambda E, hk=hk: E.tensor_tensor(
                T.skv_cols[:, hk:hk + 1], T.sk_cols[:, hk:hk + 1],
                T.ks_sb[:, hk:hk + 1], ALU.mult), inc="dve")
    for j in range(NKVC):
        ring = j % 2
        T.scol_cur = T.sv_cols[:, j:j + 1]
        w_ready = emit_wtile(ctx, T, T.wv_d, T.pwv_d, j, tau, "sign", ring, fast=True)
        wbf = T.wbf[ring]
        acc = T.pacc[j % 2]
        if tau == 0:
            ctx.wait("vector", "dve")
            ctx.op("vector", lambda E, j=j: E.tensor_tensor(
                T.svp_cols[:, j:j + 1], T.sv_cols[:, j:j + 1],
                T.vs_sb[:, j:j + 1], ALU.mult), inc="dve")
        ctx.wait("tensor", "dve", L["acc_free"][j % 2])
        for i in range(NCH):
            if i == 0:
                ctx.wait("tensor", w_ready[0], w_ready[1])
            elif i == NCH // 2 and len(w_ready) > 2:
                ctx.wait("tensor", w_ready[0], w_ready[2])
            ctx.op("tensor", lambda E, i=i, wbf=wbf, acc=acc: E.matmul(
                acc[:, :], wbf[:, i * 128:(i + 1) * 128],
                T.xn[:, i * TAU:(i + 1) * TAU],
                start=(i == 0), stop=(i == NCH - 1)), inc="pe")
        L["wbf_free"][ring] = ctx.val("pe")
        flush_pend(ctx)
        vT = T.vT_bf[j % 2]
        ctx.wait("vector", "pe")
        ctx.wait("vector", "pe", L["vT_free"][j % 2])
        ctx.op("vector", lambda E, j=j, acc=acc, vT=vT: E.tensor_scalar(
            out=vT[:, :], in0=acc[:, :], scalar1=T.svp_cols[:, j:j + 1],
            scalar2=None, op0=ALU.mult), inc="dve")
        L["acc_free"][j % 2] = ctx.val("dve")

        def vtrans(ctx, j=j, vT=vT, dve_v=ctx.val("dve"), tau=tau):
            for tt in range(TAU // 128):
                pv = T.pvt[tt % 2]
                ctx.wait("tensor", "dve", dve_v)
                ctx.wait("tensor", "dve", L["pvt_free"][tt % 2])
                ctx.op("tensor", lambda E, tt=tt, vT=vT, pv=pv: E.transpose(
                    pv[:, :], vT[:, tt * 128:(tt + 1) * 128],
                    T.ident_b[:, :]), inc="pe")
                vtm = T.vtm_bf[tt % 2]
                ctx.wait("vector", "pe")
                ctx.wait("vector", "kv", L["vtm_free"][tt % 2])
                ctx.op("vector", lambda E, vtm=vtm, pv=pv: E.tensor_copy(
                    vtm[:, :], pv[:, :]), inc="dve")
                L["pvt_free"][tt % 2] = ctx.val("dve")
                ctx.wait("gpsimd", "dve")
                ctx.dma("gpsimd",
                        T.vin_d[tau][tt * 128:(tt + 1) * 128,
                                     j * 128:(j + 1) * 128],
                        vtm[:, :], inc="kv")
                L["vtm_free"][tt % 2] = ctx.val("kv")
            L["vT_free"][j % 2] = ctx.val("pe")
        pend.append(vtrans)
    flush_pend(ctx)
    ctx.wait("gpsimd", "kv")
    ctx.op("gpsimd", lambda E, tau=tau: E.collective_compute(
        "AllGather", ALU.bypass,
        replica_groups=[[0, 1], [2, 3], [4, 5], [6, 7]],
        ins=[T.kvin_d[tau].ap()],
        outs=[T.kvout_d[tau].ap()]), inc="cc")

    for j in range(NCH):
        ring = j % 2
        T.scol_cur = T.sq_cols[:, j:j + 1]
        w_ready = emit_wtile(ctx, T, T.wq_d, T.pwq_d, j, tau, "sign", ring, fast=True)
        wbf = T.wbf[ring]
        acc = T.pacc[j % 2]
        if tau == 0:
            ctx.wait("vector", "dve")
            ctx.op("vector", lambda E, j=j: E.tensor_tensor(
                T.effq_cols[:, j:j + 1], T.sq_cols[:, j:j + 1],
                T.qs_sb[:, j:j + 1], ALU.mult))
            ctx.op("vector", lambda E, j=j: E.scalar_tensor_tensor(
                out=T.effq_cols[:, j:j + 1], in0=T.skv_cols[:, j // 4:j // 4 + 1],
                scalar=INV_SQRT_HD, in1=T.effq_cols[:, j:j + 1],
                op0=ALU.mult, op1=ALU.mult), inc="dve")
        ctx.wait("tensor", "dve", L["acc_free"][j % 2])
        for i in range(NCH):
            if i == 0:
                ctx.wait("tensor", w_ready[0], w_ready[1])
            elif i == NCH // 2 and len(w_ready) > 2:
                ctx.wait("tensor", w_ready[0], w_ready[2])
            ctx.op("tensor", lambda E, i=i, wbf=wbf, acc=acc: E.matmul(
                acc[:, :], wbf[:, i * 128:(i + 1) * 128],
                T.xn[:, i * TAU:(i + 1) * TAU],
                start=(i == 0), stop=(i == NCH - 1)), inc="pe")
        L["wbf_free"][ring] = ctx.val("pe")
        flush_pend(ctx)
        ctx.wait("vector", "pe")
        kqf = L["kq_free"][j % 2]
        if isinstance(kqf, tuple):
            ctx.wait("vector", kqf[0], kqf[1])
        else:
            ctx.wait("vector", "gd", kqf)
        kq = T.kq_bf[j % 2]
        ctx.op("vector", lambda E, j=j, acc=acc, kq=kq: E.tensor_scalar(
            out=kq[:, :], in0=acc[:, :], scalar1=T.effq_cols[:, j:j + 1],
            scalar2=None, op0=ALU.mult), inc="dve")
        dve_v = ctx.val("dve")
        ctx.wait("gpsimd", "dve", dve_v)
        ctx.dma("gpsimd", T.qTs_d[j, :, tau * TAU:(tau + 1) * TAU], kq[:, :],
                inc="gd")
        L["kq_free"][j % 2] = ("gd", ctx.val("gd"))
        ctx.wait("scalar", "dve", dve_v)
        ctx.wait("scalar", "pe", L["sqb_free"][j % 2])
        sqb = T.sqb_f[j % 2]
        ctx.op("scalar", lambda E, j=j, acc=acc, sqb=sqb: E.activation(
            out=sqb[:, :], in_=acc[:, :], func=AF.Square,
            scale=T.effq_cols[:, j:j + 1]), inc="act")
        L["acc_free"][j % 2] = ctx.val("act")
        defer_norm(j % 2, ctx.val("act"), T.maxq2)
    flush_pend(ctx)


def emit_p2(ctx, T):
    L = T.led
    negM_v = L["negM_v"]
    ctx.wait("sync", "gd")
    for tau in range(NTAU):
        ctx.wait("scalar", "cc", tau + 1)
        for hk in range(NKVC):
            for half in range(2):
                src = T.kvout_d[tau][half * 2 * KVD + hk * 128:
                                     half * 2 * KVD + hk * 128 + 128, :]
                dst = T.karena[:, hk * SEQ + half * TOK + tau * TAU:
                               hk * SEQ + half * TOK + tau * TAU + TAU]
                ctx.dma("scalar", dst, src, inc="ka")
        for hk in range(NKVC):
            for kc4 in range(8):
                half, r4 = kc4 // 4, kc4 % 4
                kc = half * 16 + tau * 4 + r4
                vv = T.kvout_d[tau][half * 2 * KVD + KVD:
                                    half * 2 * KVD + 2 * KVD, :].rearrange(
                    "(a b) t -> a (b t)", b=2)
                src = vv[r4 * 128: r4 * 128 + 128, hk * 128:(hk + 1) * 128]
                dst = T.varena[:, hk * SEQ + kc * 128:
                               hk * SEQ + kc * 128 + 128]
                ctx.dma("scalar", dst, src, inc="ka")
            L["ka_thk"][tau][hk] = ctx.val("ka")
            if tau == NTAU - 1:
                L["ka_hk"][hk] = ctx.val("ka")

    ctx.wait("scalar", "dve", negM_v)
    npair = NKC // 2
    kcseq = [half * 16 + tau * 4 + r
             for tau in range(NTAU) for half in range(2) for r in range(4)]

    def qk_pair(h, hk, qq, qr, p):
        st = T.pstp[p % 2]
        ctx.wait("tensor", "act", L["stp_free"][p % 2])
        if p % 4 == 0:
            ctx.wait("tensor", "ka", L["ka_thk"][p // 4][hk])
        for half in range(2):
            kc = kcseq[2 * p + half]
            ctx.op("tensor", lambda E, hk=hk, qq=qq, kc=kc, st=st, qr=qr,
                   half=half: E.matmul(
                       st[:, half * TAU:(half + 1) * TAU],
                       T.karena[:, hk * SEQ + kc * 128:
                                hk * SEQ + kc * 128 + 128],
                       qr[:, qq * TAU:(qq + 1) * TAU],
                       start=True, stop=True),
                   inc=("pe" if half == 1 else None))
        L["qkp_pe"][p % 2] = ctx.val("pe")

    def exp_pair(p):
        st = T.pstp[p % 2]
        pt = T.pt_bf[p % 2]
        ctx.wait("scalar", "pe",
                 max(L["qkp_pe"][p % 2], L["ptp_free"][p % 2]))
        ctx.op("scalar", lambda E, st=st, pt=pt: E.activation(
            out=pt[:, :], in_=st[:, :], func=AF.Exp,
            bias=T.negM[:, 0:1], scale=1.0), inc="act", chain=False)
        L["stp_free"][p % 2] = ctx.val("act")

    def av_pair(h, hk, qq, ot, p):
        pt = T.pt_bf[p % 2]
        ctx.wait("tensor", "act", L["stp_free"][p % 2])
        if p == 0:
            oidx = (h * NTAU + qq) % 2
            ctx.wait("tensor", "dve",
                     max(L["ot_free"][oidx], L["sums_copy"]))
        for half in range(2):
            kc = kcseq[2 * p + half]
            first = (p == 0 and half == 0)
            last = (p == npair - 1 and half == 1)
            sl_ = pt[:, half * TAU:(half + 1) * TAU]
            ctx.op("tensor", lambda E, hk=hk, kc=kc, ot=ot, sl_=sl_,
                   first=first, last=last: E.matmul(
                ot[:, :],
                T.varena[:, hk * SEQ + kc * 128: hk * SEQ + kc * 128 + 128],
                sl_, start=first, stop=last))
            ctx.op("tensor", lambda E, kc=kc, sl_=sl_, first=first, last=last:
                   E.matmul(
                T.psums0[:, :], T.ones_cb[:, :], sl_,
                start=first, stop=last),
                inc=("pe" if half == 1 else None))
        L["ptp_free"][p % 2] = ctx.val("pe")

    pending = []

    def flush_tail(ctx):
        for fn in pending:
            fn(ctx)
        del pending[:]

    def make_tail(h, qq, oidx, ot, recip_v, rec_buf):
        def tail(ctx):
            ctx.wait("tensor", "dve", recip_v)
            ctx.wait("tensor", "dve", L["prb_free"])
            ctx.op("tensor", lambda E: E.matmul(
                T.prb[:, :], T.ones_row[:, :], rec_buf[:, :],
                start=True, stop=True), inc="pe")
            ctx.wait("vector", "pe")
            ctx.op("vector", lambda E: E.tensor_copy(T.rb_sb[:, :],
                                                     T.prb[:, :]), inc="dve")
            L["prb_free"] = ctx.val("dve")
            obf = T.obf[oidx]
            ctx.wait("vector", "gd", L["obf_free"][oidx])
            ctx.op("vector", lambda E, ot=ot, obf=obf: E.tensor_tensor(
                obf[:, :], ot[:, :], T.rb_sb[:, :], ALU.mult), inc="dve")
            L["ot_free"][oidx] = ctx.val("dve")
            ctx.wait("gpsimd", "dve")
            ctx.dma("gpsimd", T.oTs_d[h, :, qq * TAU:(qq + 1) * TAU],
                    obf[:, :], inc="gd")
            L["obf_free"][oidx] = ctx.val("gd")
        return tail

    for h in range(NH):
        hk = h // 4
        qr = T.qh_bf[h % 2]
        ctx.wait("sync", "pe", L["qh_free"][h % 2])
        ctx.dma("sync", qr[:, :], T.qTs_d[h], inc="qld")
        qld_v = ctx.val("qld")
        ctx.wait("tensor", "qld", qld_v)
        for qq in range(NTAU):
            oidx = (h * NTAU + qq) % 2
            ot = T.pot[oidx]
            for p in range(npair):
                if p == 0:
                    qk_pair(h, hk, qq, qr, 0)
                    if npair > 1:
                        qk_pair(h, hk, qq, qr, 1)
                elif p + 1 < npair:
                    qk_pair(h, hk, qq, qr, p + 1)
                if p == 3:
                    flush_tail(ctx)
                exp_pair(p)
                av_pair(h, hk, qq, ot, p)
            pe_tail = ctx.val("pe")
            srow = T.sums_sb[oidx]
            rec_buf = T.rec_rows[oidx]
            ctx.wait("vector", "pe", pe_tail)
            ctx.op("vector", lambda E, srow=srow: E.tensor_copy(
                srow[:, :], T.psums0[:, :]), inc="dve")
            L["sums_copy"] = ctx.val("dve")
            ctx.op("vector", lambda E, srow=srow, rec_buf=rec_buf:
                   E.reciprocal(rec_buf[:, :], srow[:, :]), inc="dve")
            pending.append(make_tail(h, qq, oidx, ot, ctx.val("dve"), rec_buf))
        L["qh_free"][h % 2] = ctx.val("pe")
    flush_tail(ctx)


def emit_oproj(ctx, T, tau):
    L = T.led
    for j in range(NCH):
        ring = j % 2
        w_ready = emit_wtile(ctx, T, T.wo_d, T.pwo_d, j, NTAU + tau, "sign",
                             ring)
        wbf = T.wbf[ring]
        acc = T.pacc[j % 2]
        ctx.wait("tensor", w_ready[0], w_ready[1])
        ctx.wait("tensor", "dve", L["acc_free"][j % 2])
        for i in range(NCH):
            if j == 0:
                ctx.wait("tensor", "oslab", L["os_piece"][i // 8])
            ctx.op("tensor", lambda E, i=i, wbf=wbf, acc=acc: E.matmul(
                acc[:, :], wbf[:, i * 128:(i + 1) * 128],
                T.oslab[:, i * TAU:(i + 1) * TAU],
                start=(i == 0), stop=(i == NCH - 1)), inc="pe")
        L["wbf_free"][ring] = ctx.val("pe")
        ctx.wait("vector", "pe")
        ctx.wait("vector", "xld", L["x1_piece"][j // 8])
        ev = T.ev_f[j % 2]
        ctx.op("vector", lambda E, j=j, acc=acc, ev=ev: E.tensor_scalar(
            out=ev[:, :], in0=acc[:, :], scalar1=T.effo_cols[:, j:j + 1],
            scalar2=None, op0=ALU.mult))
        ctx.op("vector", lambda E, j=j, ev=ev: E.tensor_tensor(
            T.slab[:, j * TAU:(j + 1) * TAU], T.slab[:, j * TAU:(j + 1) * TAU],
            ev[:, :], ALU.add), inc="dve")
        L["acc_free"][j % 2] = ctx.val("dve")
        L["x1_chunk"][j] = ctx.val("dve")



def _walk(ctx, T, stage):
    nc = ctx.nc
    nm = lambda s: f"{s}_{ctx.active}"

    ctx.dma("sync", T.ident_f[:, :], T.idf_d[:, :], inc="xld")
    ctx.dma("sync", T.ident_b[:, :], T.idb_d[:, :], inc="xld")
    ctx.dma("sync", T.ones_row[:, :], T.onesrow_d[:, :], inc="xld")
    ctx.dma("sync", T.invd[:, :], T.invd_d[:, :], inc="xld")
    ctx.dma("sync", T.invdb[:, :], T.invdb_d[:, :], inc="xld")
    ctx.dma("sync", T.ones_cb[:, :], T.onescb_d[:, :], inc="xld")
    ctx.dma("sync", T.pat2[:, :], T.pat2_d[:, :], inc="xld")
    ctx.dma("gpsimd", T.wa_sb[:, :], T.wa_d[:, :], inc="gd")
    ctx.dma("gpsimd", T.wbp_sb[:, :], T.wbp_d[:, :], inc="gd")
    ctx.dma("sync", T.wgf_sb[:, :], T.wg_d[:, :], inc="xld")
    ctx.dma("sync", T.ones_cf[:, :], T.onescf_d[:, :], inc="xld")
    ctx.dma("sync", T.qs_sb[:, :], T.qs_d[:, :], inc="xld")
    ctx.dma("sync", T.ks_sb[:, :], T.ks_d[:, :], inc="xld")
    ctx.dma("sync", T.vs_sb[:, :], T.vs_d[:, :], inc="xld")
    ctx.dma("sync", T.osc_sb[:, :], T.os_d[:, :], inc="xld")
    ctx.dma("sync", T.hbb_sb[:, :], T.bb_d[:, :], inc="xld")
    setup_xld = ctx.val("xld")
    setup_gd = ctx.val("gd")
    ctx.wait("vector", "xld", setup_xld)
    ctx.wait("vector", "gd", setup_gd)
    ctx.op("vector", lambda E: E.tensor_scalar(
        out=T.hbb_sb[:, :], in0=T.hbb_sb[:, :], scalar1=0.5, scalar2=None,
        op0=ALU.mult))
    ctx.op("vector", lambda E: E.memset(T.maxq2[:, :], 0.0))
    ctx.op("vector", lambda E: E.memset(T.maxk2[:, :], 0.0), inc="dve")
    setup_dve = ctx.val("dve")
    for e in ("tensor", "scalar", "gpsimd", "sync"):
        ctx.wait(e, "dve", setup_dve)
        ctx.wait(e, "xld", setup_xld)

    T.led = {
        "sq_free": [0, 0], "sqb_free": [0, 0], "acc_free": [0, 0],
        "wload_free": [0, 0], "wbf_free": [0, 0], "wcast_free": 0,
        "ptr_free": 0, "pvt_free": [0, 0], "kq_free": [0, 0],
        "wbf_gd": [0, 0],
        "vtm_free": [0, 0], "vT_free": [0, 0], "xn_ready": 0,
        "sl_ready": 0,
        "xn_chunk": [0] * NCH, "bc_ready": 0, "pglog_free": 0,
        "xb_free": [0, 0],
        "x1_chunk": [0] * NCH, "x1_piece": [0] * 4, "negM_v": 0,
        "x1_ready": 0, "cc_done": 0, "pm_free": 0, "prep_gd": 0,
        "slab_free": 0, "slab_free_gd": 0,
    }
    L = T.led
    T.wload_single = False

    with contextlib.ExitStack() as stk:
        sb = lambda n_, sh, dt: stk.enter_context(
            nc.sbuf_tensor(nm(n_), sh, dt))
        psb = lambda n_, sh, dt=F32: stk.enter_context(
            nc.psum_tensor(nm(n_), sh, dt))
        T.slab = sb("slab", [128, NCH * TAU], F32)
        T.xn = sb("xn", [128, NCH * TAU], BF16)
        T.wload = [sb("wload0", [128, D], F32), sb("wload1", [128, D], F32)]
        T.wbf = [sb("wbf0", [128, D], BF16), sb("wbf1", [128, D], BF16)]
        T.sh2 = sb("sh2", [128, 2], F32)
        T.sqb_f = [sb("sqb0", [128, TAU], BF16), sb("sqb1", [128, TAU], BF16)]
        T.xbf_f = [sb("xbf0", [128, TAU], BF16), sb("xbf1", [128, TAU], BF16)]
        T.bc_r = sb("bc_r", [128, TAU], F32)
        T.bc_mr = sb("bc_mr", [128, TAU], F32)
        tt1 = sb("ttf0", [128, TAU], F32)
        T.tt_f = [tt1, tt1]
        T.ev_f = T.tt_f
        T.bt = sb("bt", [128, 2 * TAU], F32)
        T.gp_norm = 8
        T.kq_bf = [sb("kq0", [128, TAU], BF16), sb("kq1", [128, TAU], BF16)]
        T.vT_bf = [sb("vT_bf0", [128, TAU], BF16),
                   sb("vT_bf1", [128, TAU], BF16)]
        T.vtm_bf = [sb("vtm0", [128, 128], BF16), sb("vtm1", [128, 128], BF16)]
        T.low_sb = sb("low_sb", [128, TAU], F32)
        T.sl_bf = sb("sl_bf", [128, TAU], BF16)
        T.glog_sb = sb("glog_sb", [8, TAU], F32)
        T.logtm = sb("logtm", [128, 32], F32)
        T.wfT = sb("wfT", [8, TAU], F32)
        T.max1 = sb("max1", [128, 4], F32)
        T.max2 = sb("max2", [128, 4], F32)
        T.e2 = sb("e2c", [128, 4], F32)
        T.w1 = sb("w1c", [128, 4], F32)
        T.w2 = sb("w2c", [128, 4], F32)
        T.zrow = sb("zrow", [128, 4], F32)
        T.eq1 = sb("eq1", [128, 32], F32)
        T.l2 = sb("l2t", [128, 32], F32)
        T.wf = sb("wft", [128, 32], F32)
        T.m_row = sb("m_row", [1, TAU], F32)
        T.v_row = sb("v_row", [1, TAU], F32)
        T.r_row = sb("r_row", [1, TAU], F32)
        T.mr_row = sb("mr_row", [1, TAU], F32)
        T.pacc = [psb("pacc0", [128, TAU]), psb("pacc1", [128, TAU])]
        T.pm = psb("pm", [1, TAU])
        T.ps = psb("ps", [1, TAU])
        T.pmisc = psb("pmisc", [128, TAU])
        T.pglog = T.pmisc[0:8, :]
        T.ptr = psb("ptr", [128, 128])
        T.pvt = [psb("pvt0", [128, 128], BF16),
                 psb("pvt1", [128, 128], BF16)]

        for i in range(NCH):
            ctx.op("tensor", lambda E, i=i: E.matmul(
                T.pglog[:, 0:1], T.wgf_sb[:, i * 8:(i + 1) * 8],
                T.ones_cf[:, :], start=(i == 0), stop=(i == NCH - 1)),
                inc="pe")
        ctx.wait("vector", "pe")
        ctx.op("vector", lambda E: E.tensor_copy(
            T.gws_col[:, :], T.pglog[0:8, 0:1]), inc="dve")

        for tau in range(NTAU):
            T.cur_tau = tau
            ctx.wait("scalar", "dve", L["slab_free"])
            ctx.wait("scalar", "gd", L["slab_free_gd"])
            for pc in range(4):
                ctx.dma("scalar",
                        T.slab[:, pc * 8 * TAU:(pc + 1) * 8 * TAU].rearrange(
                            "p (c t) -> p c t", c=8),
                        T.xT_d[pc * 8:(pc + 1) * 8, :,
                               tau * TAU:(tau + 1) * TAU].rearrange(
                            "c p t -> p c t"), inc="xld")
                L["x1_piece"][pc] = ctx.val("xld")
            emit_ln_stats(ctx, T, lambda c: ("xld", L["x1_piece"][c // 8]))
            emit_gate_top2(ctx, T)
            emit_ln_norm(ctx, T)
            emit_low_sl(ctx, T)
            emit_ffn_base(ctx, T, tau)
            ctx.wait("gpsimd", "dve", L["x1_ready"])
            ctx.dma("gpsimd",
                    T.x1s_d[:, :, tau * TAU:(tau + 1) * TAU].rearrange(
                        "c p t -> p c t"),
                    T.slab[:, :].rearrange("p (c t) -> p c t", c=NCH),
                    inc="gd")
            x1dma_v = ctx.val("gd")
            if stage == "ffn1":
                L["slab_free"] = L["x1_ready"]
                L["slab_free_gd"] = x1dma_v
                L["prep_gd"] = ctx.val("gd")
                continue
            emit_ln_stats(ctx, T, lambda c: ("dve", L["x1_chunk"][c]))
            if tau > 0:
                lo = [0, 0, 11, 22][tau]
                hi = [0, 11, 22, NCH][tau]
                for j in range(lo, hi):
                    ring = j % 2
                    T.scol_cur = T.so_cols[:, j:j + 1]
                    emit_wtile(ctx, T, T.wo_d, T.pwo_d, j, 0, "sign", ring)
                    L["wbf_free"][ring] = ctx.val("pe")
                    ctx.op("vector", lambda E, j=j: E.tensor_tensor(
                        T.effo_cols[:, j:j + 1], T.so_cols[:, j:j + 1],
                        T.osc_sb[:, j:j + 1], ALU.mult))
            emit_ln_norm(ctx, T)
            emit_qkv(ctx, T, tau)
            L["slab_free"] = ctx.val("dve")
            L["slab_free_gd"] = x1dma_v
            L["prep_gd"] = ctx.val("gd")
        L["cc_done"] = ctx.val("cc")
        if stage == "full":
            ctx.op("vector", lambda E: E.tensor_tensor(
                T.t11[:, :], T.maxq2[:, :], T.maxk2[:, :], ALU.mult),
                inc="dve")
            ctx.wait("scalar", "dve")
            ctx.op("scalar", lambda E: E.activation(
                out=T.t11b[:, :], in_=T.t11[:, :], func=AF.Ln))
            ctx.op("scalar", lambda E: E.activation(
                out=T.t11[:, :], in_=T.t11b[:, :], func=AF.Exp, scale=0.5),
                inc="act")
            ctx.wait("vector", "act")
            ctx.op("vector", lambda E: E.tensor_scalar(
                out=T.t11[:, :], in0=T.t11[:, :], scalar1=-1.0, scalar2=None,
                op0=ALU.mult), inc="dve")
            ctx.wait("tensor", "dve")
            ctx.wait("tensor", "dve", L["ptr_free"])
            ctx.op("tensor", lambda E: E.matmul(
                T.ptr[:, 0:1], T.ones_row[:, :], T.t11[:, :],
                start=True, stop=True), inc="pe")
            ctx.wait("vector", "pe")
            ctx.op("vector", lambda E: E.tensor_copy(
                T.negM[:, :], T.ptr[:, 0:1]), inc="dve")
            L["negM_v"] = ctx.val("dve")

    if stage in ("ffn1", "p1"):
        ctx.barrier()
        ctx.wait("gpsimd", "gd")
        if T.kdbg_d is not None:
            ctx.wait("gpsimd", "cc", ctx.val("cc"))
            for t in range(NTAU):
                for half in range(2):
                    ctx.dma("gpsimd",
                            T.kdbg_d[t][half * KVD:(half + 1) * KVD, :],
                            T.kvout_d[t][half * 2 * KVD:
                                         half * 2 * KVD + KVD, :], inc="out")
                    vv = T.kvout_d[t][half * 2 * KVD + KVD:
                                      half * 2 * KVD + 2 * KVD, :].rearrange(
                        "(a b) x -> a (b x)", b=2)
                    ctx.dma("gpsimd",
                            T.vdbg_d[t][half * TAU:(half + 1) * TAU, :],
                            vv, inc="out")
        ctx.dma("gpsimd", T.yT_d[:, :, :], T.x1s_d[:, :, :], inc="out")
        ctx.wait("gpsimd", "out", ctx.val("out"))
        return

    ctx.barrier()

    with contextlib.ExitStack() as stk:
        sb = lambda n_, sh, dt: stk.enter_context(
            nc.sbuf_tensor(nm(n_), sh, dt))
        psb = lambda n_, sh, dt=F32: stk.enter_context(
            nc.psum_tensor(nm(n_), sh, dt))
        T.karena = sb("karena", [128, NKVC * SEQ], BF16)
        T.varena = sb("varena", [128, NKVC * SEQ], BF16)
        T.qh_bf = [sb("qh0", [128, TOK], BF16), sb("qh1", [128, TOK], BF16)]
        T.pt_bf = [sb(f"ptp{i}", [128, 2 * TAU], BF16) for i in range(2)]
        T.rb_sb = sb("rb_sb", [128, TAU], F32)
        T.obf = [sb("obf0", [128, TAU], BF16), sb("obf1", [128, TAU], BF16)]
        T.sums_sb = [sb("sums0", [1, TAU], F32), sb("sums1", [1, TAU], F32)]
        T.rec_rows = [sb("rec_row0", [1, TAU], F32),
                      sb("rec_row1", [1, TAU], F32)]
        T.pstp = [stk.enter_context(nc.psum_tensor(
            nm(f"pstp{i}"), [128, 2 * TAU], F32, unsafe=True)) for i in range(2)]
        T.pot = [psb(f"pot{i}", [128, TAU]) for i in range(2)]
        T.psums0 = psb("psums0", [1, TAU])
        T.prb = psb("prb", [128, TAU])
        L.update({
            "stp_free": [0, 0], "ptp_free": [0, 0], "qkp_pe": [0, 0],
            "ot_free": [0, 0], "sums_copy": 0, "obf_free": [0, 0],
            "qh_free": [0, 0], "prb_free": 0, "ka_hk": [0] * NKVC,
            "ka_thk": [[0] * NKVC for _ in range(NTAU)],
        })
        emit_p2(ctx, T)

    ctx.barrier()

    if stage == "p2":
        ctx.wait("gpsimd", "gd")
        ctx.dma("gpsimd", T.yT_d[:, :, :], T.oTs_d[:, :, :], inc="out")
        ctx.wait("gpsimd", "out", ctx.val("out"))
        return

    with contextlib.ExitStack() as stk:
        sb = lambda n_, sh, dt: stk.enter_context(
            nc.sbuf_tensor(nm(n_), sh, dt))
        psb = lambda n_, sh, dt=F32: stk.enter_context(
            nc.psum_tensor(nm(n_), sh, dt))
        T.slab = sb("slab3", [128, NCH * TAU], F32)
        T.oslab = sb("oslab", [128, NCH * TAU], BF16)
        T.xn = sb("xn3", [128, NCH * TAU], BF16)
        T.wload = [None, None]
        T.wload_single = True
        T.wbf = [sb("wbf30", [128, D], BF16), sb("wbf31", [128, D], BF16)]
        T.wcast = None
        T.sqb_f = [sb("sqb30", [128, TAU], BF16),
                   sb("sqb31", [128, TAU], BF16)]
        T.xbf_f = [sb("xbf30", [128, TAU], BF16), sb("xbf31", [128, TAU], BF16)]
        T.bc_r = sb("bc_r3", [128, TAU], F32)
        T.bc_mr = sb("bc_mr3", [128, TAU], F32)
        tt3 = sb("ttf30", [128, TAU], F32)
        T.tt_f = [tt3, tt3]
        T.ev_f = T.tt_f
        T.bt = sb("bt3", [128, 2 * TAU], F32)
        T.gp_norm = 8
        T.low_sb = sb("low_sb3", [128, TAU], F32)
        T.sl_bf = sb("sl_bf3", [128, TAU], BF16)
        T.glog_sb = sb("glog3", [8, TAU], F32)
        T.logtm = sb("logtm3", [128, 32], F32)
        T.wfT = sb("wfT3", [8, TAU], F32)
        T.max1 = sb("max13", [128, 4], F32)
        T.max2 = sb("max23", [128, 4], F32)
        T.e2 = sb("e2c3", [128, 4], F32)
        T.w1 = sb("w1c3", [128, 4], F32)
        T.w2 = sb("w2c3", [128, 4], F32)
        T.zrow = sb("zrow3", [128, 4], F32)
        T.eq1 = sb("eq13", [128, 32], F32)
        T.l2 = sb("l2t3", [128, 32], F32)
        T.wf = sb("wft3", [128, 32], F32)
        T.m_row = sb("m_row3", [1, TAU], F32)
        T.v_row = sb("v_row3", [1, TAU], F32)
        T.r_row = sb("r_row3", [1, TAU], F32)
        T.mr_row = sb("mr_row3", [1, TAU], F32)
        T.pacc = [psb("pacc30", [128, TAU]), psb("pacc31", [128, TAU])]
        T.pm = psb("pm3", [1, TAU])
        T.ps = psb("ps3", [1, TAU])
        T.pmisc = psb("pmisc3", [128, TAU])
        T.pglog = T.pmisc[0:8, :]
        T.ptr = psb("ptr3", [128, 128])
        L.update({
            "sq_free": [0, 0], "acc_free": [0, 0], "wload_free": [0, 0],
            "wbf_free": [0, 0], "wcast_free": 0, "ptr_free": 0,
            "slab_free": 0, "slab_free_gd": 0, "oslab_free": 0,
            "wbf_gd": [0, 0], "os_piece": [0] * 4,
            "oslab_ready": 0, "out_prev": 0,
        })

        ctx.wait("scalar", "gd")
        for tau in range(NTAU):
            ctx.wait("scalar", "dve", L["slab_free"])
            ctx.wait("scalar", "out", L["out_prev"])
            for pc in range(4):
                ctx.dma("scalar",
                        T.slab[:, pc * 8 * TAU:(pc + 1) * 8 * TAU].rearrange(
                            "p (c t) -> p c t", c=8),
                        T.x1s_d[pc * 8:(pc + 1) * 8, :,
                                tau * TAU:(tau + 1) * TAU].rearrange(
                            "c p t -> p c t"), inc="xld")
                L["x1_piece"][pc] = ctx.val("xld")
            x1_v = ctx.val("xld")
            ctx.wait("scalar", "pe", L["oslab_free"])
            for pc in range(4):
                ctx.dma("scalar",
                        T.oslab[:, pc * 8 * TAU:(pc + 1) * 8 * TAU].rearrange(
                            "p (c t) -> p c t", c=8),
                        T.oTs_d[pc * 8:(pc + 1) * 8, :,
                                tau * TAU:(tau + 1) * TAU].rearrange(
                            "c p t -> p c t"), inc="oslab")
                L["os_piece"][pc] = ctx.val("oslab")
            L["oslab_ready"] = ctx.val("oslab")
            emit_oproj(ctx, T, tau)
            L["oslab_free"] = ctx.val("pe")
            emit_ln_stats(ctx, T, lambda c: ("dve", L["x1_chunk"][c]))
            emit_gate_top2(ctx, T)
            emit_ln_norm(ctx, T)
            emit_low_sl(ctx, T)
            emit_ffn_base(ctx, T, NTAU + tau)
            if tau == 0:
                L["prep_gd"] = ctx.val("gd")
            ctx.wait("gpsimd", "dve", L["x1_ready"])
            ctx.dma("gpsimd",
                    T.yT_d[:, :, tau * TAU:(tau + 1) * TAU].rearrange(
                        "c p t -> p c t"),
                    T.slab[:, :].rearrange("p (c t) -> p c t", c=NCH),
                    inc="out")
            L["slab_free"] = ctx.val("dve")
            L["out_prev"] = ctx.val("out")
        ctx.wait("gpsimd", "out", ctx.val("out"))
        ctx.wait("sync", "out", ctx.val("out"))


def build_nc(stage=None, dbg=None):
    stage = STAGE if stage is None else stage
    dbg = DBG if dbg is None else dbg
    nc = bass.Bass("TRN2", target_bir_lowering=False)
    T = Tset = type("Tset", (), {})()

    def din(name, shape, dt=F32):
        return nc.dram_tensor(name, shape, dt, kind="ExternalInput")

    def dsc(name, shape, dt):
        if dbg:
            return nc.dram_tensor(name, shape, dt, kind="ExternalOutput")
        return nc.dram_tensor(name, shape, dt)

    T.xT_d = din("xT", [NCH, 128, TOK])
    T.wq_d = din("wq", [NCH, 128, D])
    T.wk_d = din("wk", [NKVC, 128, D])
    T.wv_d = din("wv", [NKVC, 128, D])
    T.wo_d = din("wo", [NCH, 128, D])
    T.wb_d = din("wb", [NCH, 128, D])
    T.wa_d = din("wa", [128, D])
    T.wbp_d = din("wbp", [128, D])
    T.wg_d = din("wg", [128, NCH * 8])
    T.onescf_d = din("onescf", [128, 1])
    T.qs_d = din("qs", [128, NCH])
    T.ks_d = din("ks", [128, NKVC])
    T.vs_d = din("vs", [128, NKVC])
    T.os_d = din("osc", [128, NCH])
    T.bb_d = din("bb", [128, NCH])
    T.idf_d = din("idf", [128, 128])
    T.idb_d = din("idb", [128, 128], BF16)
    T.onesrow_d = din("onesrow", [1, 128])
    T.invd_d = din("invd", [128, 1])
    T.invdb_d = din("invdb", [128, 1], BF16)
    T.onescb_d = din("onescb", [128, 1], BF16)
    T.pat2_d = din("pat2", [8, 128])

    T.pwq_d = nc.dram_tensor("pwq", [NCH, 128, D], BF16)
    T.pwk_d = nc.dram_tensor("pwk", [NKVC, 128, D], BF16)
    T.pwv_d = nc.dram_tensor("pwv", [NKVC, 128, D], BF16)
    T.pwo_d = nc.dram_tensor("pwo", [NCH, 128, D], BF16)
    T.pwb_d = nc.dram_tensor("pwb", [NCH, 128, D], BF16)
    T.x1s_d = dsc("x1s", [NCH, 128, TOK], F32)
    T.qTs_d = dsc("qTs", [NH, 128, TOK], BF16)
    T.oTs_d = dsc("oTs", [NH, 128, TOK], BF16)
    T.kvin_d = [nc.dram_tensor(f"kvin{t}", [2 * KVD, TAU], BF16)
                for t in range(NTAU)]
    T.kin_d = [kv[0:KVD, :] for kv in T.kvin_d]
    T.vin_d = [kv[KVD:2 * KVD, :].rearrange("(a b) t -> a (b t)", b=2)
               for kv in T.kvin_d]
    T.kvout_d = [nc.dram_tensor(f"kvout{t}", [4 * KVD, TAU], BF16)
                 for t in range(NTAU)]
    T.kvo_k = [kv for kv in T.kvout_d]
    T.kvo_v = [kv for kv in T.kvout_d]
    T.kdbg_d = T.vdbg_d = None
    if dbg:
        T.kdbg_d = [nc.dram_tensor(f"kdbg{t}", [2 * KVD, TAU], BF16,
                                   kind="ExternalOutput") for t in range(NTAU)]
        T.vdbg_d = [nc.dram_tensor(f"vdbg{t}", [2 * TAU, KVD], BF16,
                                   kind="ExternalOutput") for t in range(NTAU)]
    T.yT_d = nc.dram_tensor("yT", [NCH, 128, TOK], F32, kind="ExternalOutput")
    T.dump = None
    T.cur_tau = -1
    if dbg:
        T.dump = {
            "glog": nc.dram_tensor("dglog", [8, TAU], F32, kind="ExternalOutput"),
            "wfT": nc.dram_tensor("dwfT", [8, TAU], F32, kind="ExternalOutput"),
            "low": nc.dram_tensor("dlow", [128, TAU], F32, kind="ExternalOutput"),
            "sl": nc.dram_tensor("dsl", [128, TAU], BF16, kind="ExternalOutput"),
            "logtm": nc.dram_tensor("dlogtm", [128, 8], F32, kind="ExternalOutput"),
            "wf": nc.dram_tensor("dwf", [128, 8], F32, kind="ExternalOutput"),
            "max1": nc.dram_tensor("dmax1", [128, 1], F32, kind="ExternalOutput"),
            "eq1": nc.dram_tensor("deq1", [128, 8], F32, kind="ExternalOutput"),
            "w1": nc.dram_tensor("dw1", [128, 1], F32, kind="ExternalOutput"),
        }

    sem_names = ["gd", "wld", "wl2", "xld", "act", "dve", "pe", "cc", "ka",
                 "qld", "oslab", "out", "bar", "kv", "gp"]

    with contextlib.ExitStack() as st:
        T.ident_f = st.enter_context(nc.sbuf_tensor("ident_f", [128, 128], F32))
        T.ident_b = st.enter_context(nc.sbuf_tensor("ident_b", [128, 128], BF16))
        T.ones_row = st.enter_context(nc.sbuf_tensor("ones_row", [1, 128], F32))
        T.invd = st.enter_context(nc.sbuf_tensor("invd_s", [128, 1], F32))
        T.invdb = st.enter_context(nc.sbuf_tensor("invdb_s", [128, 1], BF16))
        T.ones_cb = st.enter_context(nc.sbuf_tensor("ones_cb", [128, 1], BF16))
        T.pat2 = st.enter_context(nc.sbuf_tensor("pat2_s", [8, 128], F32))
        T.wa_sb = st.enter_context(nc.sbuf_tensor("wa_sb", [128, D], BF16))
        T.wbp_sb = st.enter_context(nc.sbuf_tensor("wbp_sb", [128, D], BF16))
        T.wgf_sb = st.enter_context(nc.sbuf_tensor("wgf_sb", [128, NCH * 8], F32))
        T.gws_col = st.enter_context(nc.sbuf_tensor("gws_col", [8, 1], F32))
        T.ones_cf = st.enter_context(nc.sbuf_tensor("ones_cf", [128, 1], F32))
        T.qs_sb = st.enter_context(nc.sbuf_tensor("qs_sb", [128, NCH], F32))
        T.ks_sb = st.enter_context(nc.sbuf_tensor("ks_sb", [128, NKVC], F32))
        T.vs_sb = st.enter_context(nc.sbuf_tensor("vs_sb", [128, NKVC], F32))
        T.osc_sb = st.enter_context(nc.sbuf_tensor("osc_sb", [128, NCH], F32))
        T.hbb_sb = st.enter_context(nc.sbuf_tensor("hbb_sb", [128, NCH], F32))
        T.sq_cols = st.enter_context(nc.sbuf_tensor("sq_cols", [128, NCH], F32))
        T.sk_cols = st.enter_context(nc.sbuf_tensor("sk_cols", [128, NKVC], F32))
        T.sv_cols = st.enter_context(nc.sbuf_tensor("sv_cols", [128, NKVC], F32))
        T.so_cols = st.enter_context(nc.sbuf_tensor("so_cols", [128, NCH], F32))
        T.skv_cols = st.enter_context(nc.sbuf_tensor("skv_cols", [128, NKVC], F32))
        T.svp_cols = st.enter_context(nc.sbuf_tensor("svp_cols", [128, NKVC], F32))
        T.effq_cols = st.enter_context(nc.sbuf_tensor("effq_cols", [128, NCH], F32))
        T.effo_cols = st.enter_context(nc.sbuf_tensor("effo_cols", [128, NCH], F32))
        T.maxq2 = st.enter_context(nc.sbuf_tensor("maxq2", [1, 1], F32))
        T.maxk2 = st.enter_context(nc.sbuf_tensor("maxk2", [1, 1], F32))
        T.t11 = st.enter_context(nc.sbuf_tensor("t11", [1, 1], F32))
        T.t11b = st.enter_context(nc.sbuf_tensor("t11b", [1, 1], F32))
        T.negM = st.enter_context(nc.sbuf_tensor("negM", [128, 1], F32))
        T.junk = st.enter_context(nc.sbuf_tensor("junk", [128, 128], F32))
        sems = {s: _Sem(st.enter_context(nc.semaphore(s))) for s in sem_names}

        with nc.Block() as blk:
            for eng in ENGINES:
                def section(E, eng=eng):
                    ctx = Ctx(nc, eng, {k: _Sem(v.h) for k, v in sems.items()})
                    _walk(ctx, T, stage)
                getattr(blk, eng)(section)
    return nc, T



def _wtile(W):
    ofn = W.shape[0] // 128
    t = W.T.reshape(NCH, 128, ofn, 128).transpose(2, 1, 0, 3)
    return np.ascontiguousarray(t.reshape(ofn, 128, D))


def prep_inputs(inputs):
    x = np.asarray(inputs["L0_attn_hidden_states"], dtype=np.float32)
    wq = _wtile(np.asarray(inputs["L0_attn_q_weight"], np.float32))
    wk = _wtile(np.asarray(inputs["L0_attn_k_weight"], np.float32))
    wv = _wtile(np.asarray(inputs["L0_attn_v_weight"], np.float32))
    wo = _wtile(np.asarray(inputs["L0_attn_o_weight"], np.float32))
    wb = _wtile(np.asarray(inputs["L0_ffn_base_w"], np.float32))
    la = np.asarray(inputs["L0_ffn_lora_A"], np.float32).reshape(128, D)
    wa = np.ascontiguousarray(
        la.T.reshape(NCH, 128, 128).transpose(1, 0, 2).reshape(128, D))
    wbp = np.ascontiguousarray(
        np.asarray(inputs["L0_ffn_lora_B"], np.float32).transpose(0, 2, 1)
        .reshape(128, D))
    wg = np.ascontiguousarray(
        np.asarray(inputs["L0_ffn_gate_w"], np.float32).T
        .reshape(NCH, 128, 8).transpose(1, 0, 2).reshape(128, NCH * 8))

    def cols(v, n):
        return np.ascontiguousarray(np.asarray(v, np.float32).reshape(n, 128).T)

    qs = cols(inputs["L0_attn_q_scale"], NCH)
    ks = cols(inputs["L0_attn_k_scale"], NKVC)
    vs = cols(inputs["L0_attn_v_scale"], NKVC)
    osc = cols(inputs["L0_attn_o_scale"], NCH)
    bb = cols(inputs["L0_ffn_base_b"], NCH)

    import ml_dtypes
    idf = np.eye(128, dtype=np.float32)
    idb = np.eye(128, dtype=ml_dtypes.bfloat16)
    onesrow = np.ones((1, 128), np.float32)
    invd = np.full((128, 1), -1.0 / D, np.float32)
    invdb = np.full((128, 1), -1.0 / D, ml_dtypes.bfloat16)
    onescb = np.ones((128, 1), ml_dtypes.bfloat16)
    pat2 = np.zeros((8, 128), np.float32)
    for e in range(8):
        pat2[e, e * 16:(e + 1) * 16] = 2.0

    shared = dict(wq=wq, wk=wk, wv=wv, wo=wo, wb=wb, wa=wa, wbp=wbp, wg=wg,
                  qs=qs, ks=ks, vs=vs, osc=osc, bb=bb, idf=idf, idb=idb,
                  onesrow=onesrow, invd=invd, invdb=invdb, onescb=onescb,
                  onescf=np.ones((128, 1), np.float32), pat2=pat2)
    in_maps = []
    for c in range(8):
        b, half = c // 2, c % 2
        xc = x[b, half * TOK:(half + 1) * TOK, :]
        xT = np.ascontiguousarray(xc.T).reshape(NCH, 128, TOK)
        m = dict(shared)
        m["xT"] = xT
        in_maps.append(m)
    return in_maps


def _enable_ldw_opt():
    try:
        from concourse.compiler_utils import (get_compiler_flags,
                                              set_compiler_flags)
        flags = [f.replace("--enable-ldw-opt=false", "--enable-ldw-opt=true")
                 for f in get_compiler_flags()]
        set_compiler_flags(flags)
    except Exception:
        pass


def run_cores(inputs, stage=None, dbg=None, **kw):
    _enable_ldw_opt()
    in_maps = prep_inputs(inputs)
    nc, _ = build_nc(stage=stage, dbg=dbg)
    return run_bass_kernel_spmd(nc, in_maps, core_ids=list(range(8)), **kw)


def kernel(**inputs):
    res = run_cores(inputs)
    y = np.empty((4, SEQ, D), np.float32)
    for c in range(8):
        b, half = c // 2, c % 2
        yT = np.asarray(res.results[c]["yT"], np.float32).reshape(D, TOK)
        y[b, half * TOK:(half + 1) * TOK, :] = yT.T
    return y


# version 30
# speedup vs baseline: 1.1620x; 1.1620x over previous
import contextlib
import numpy as np
import concourse.bass as bass
import concourse.mybir as mybir
from concourse.bass_utils import run_bass_kernel_spmd
from concourse.alu_op_type import AluOpType as ALU

F32 = mybir.dt.float32
F32R = mybir.dt.float32r
BF16 = mybir.dt.bfloat16


def r32(ap):
    return ap.bitcast(mybir.dt.float32r)
AF = mybir.ActivationFunctionType
AX = mybir.AxisListType

D = 4096
NCH = 32
TOK = 2048
TAU = 512
NTAU = TOK // TAU
KVD = 1024
NKVC = 8
NH = 32
SEQ = 4096
NKC = SEQ // 128
EPS = 1e-5
INV_SQRT_HD = float(1.0 / np.sqrt(128.0))

STAGE = "full"
DBG = False
CACHE_W = True

ENGINES = ("tensor", "vector", "scalar", "gpsimd", "sync")


class _Sem:
    __slots__ = ("h", "n")

    def __init__(self, h):
        self.h = h
        self.n = 0


class Ctx:
    def __init__(self, nc, active, sems):
        self.nc = nc
        self.active = active
        self.sems = sems

    AUTO = {"vector": "dve", "scalar": "act"}

    def op(self, eng, fn, inc=None, inc_n=1, chain=True):
        auto = chain and eng in self.AUTO
        if auto:
            s = self.AUTO[eng]
            if inc is None:
                inc, inc_n = s, 1
            assert inc == s and inc_n == 1, (eng, inc)
            self.wait(eng, s, self.sems[s].n)
        if eng == self.active:
            inst = fn(getattr(self.nc, eng))
            if inc is not None:
                inst.then_inc(self.sems[inc].h, inc_n)
        if inc is not None:
            self.sems[inc].n += inc_n

    def dma(self, eng, out, in_, inc, **kw):
        self.op(eng, lambda E: E.dma_start(out=out, in_=in_, **kw),
                inc=inc, inc_n=16, chain=False)

    def wait(self, eng, sem, val=None):
        v = self.sems[sem].n if val is None else val
        if eng == self.active:
            getattr(self.nc, eng).wait_ge(self.sems[sem].h, v)

    def val(self, sem):
        return self.sems[sem].n

    def barrier(self):
        for e in ENGINES:
            if e in self.AUTO:
                self.wait(e, self.AUTO[e], self.sems[self.AUTO[e]].n)
            self.op(e, lambda E: E.nop(), inc="bar", inc_n=1, chain=False)
        tgt = self.val("bar")
        for e in ENGINES:
            self.wait(e, "bar", tgt)



def wait_xc(ctx, eng, xc):
    if isinstance(xc, tuple):
        ctx.wait(eng, xc[0], xc[1])
    else:
        ctx.wait(eng, "dve", xc)


def emit_ln(ctx, T, wait_slab):
    emit_ln_stats(ctx, T, wait_slab)
    emit_ln_norm(ctx, T)


def emit_ln_stats(ctx, T, wait_slab):
    L = T.led
    wfn = wait_slab if callable(wait_slab) else (lambda c: wait_slab)
    for c in range(NCH):
        xb = T.xbf_f[c % 2]
        sq = T.sqb_f[c % 2]
        ws = wfn(c)
        ctx.wait("scalar", ws[0], ws[1])
        ctx.wait("scalar", "pe", L["sq_free"][c % 2])
        ctx.wait("vector", ws[0], ws[1])
        ctx.wait("vector", "pe", L["xb_free"][c % 2])
        if c == 0:
            ctx.wait("tensor", "dve", L["pm_free"])
            ctx.wait("tensor", "act")
        ctx.op("vector", lambda E, c=c, xb=xb: E.tensor_copy(
            xb[:, :], T.slab[:, c * TAU:(c + 1) * TAU]), inc="dve")
        ctx.op("scalar", lambda E, c=c, sq=sq: E.activation(
            out=sq[:, :], in_=T.slab[:, c * TAU:(c + 1) * TAU], func=AF.Square),
            inc="act")
        ctx.wait("tensor", "dve")
        ctx.op("tensor", lambda E, c=c, xb=xb: E.matmul(
            T.pm[:, :], T.invdb[:, :], xb[:, :],
            start=(c == 0), stop=(c == NCH - 1)))
        ctx.wait("tensor", "act")
        ctx.op("tensor", lambda E, c=c, sq=sq: E.matmul(
            T.ps[:, :], T.invdb[:, :], sq[:, :],
            start=(c == 0), stop=(c == NCH - 1)), inc="pe")
        L["sq_free"][c % 2] = ctx.val("pe")
        L["xb_free"][c % 2] = ctx.val("pe")
    ctx.wait("vector", "pe")
    ctx.op("vector", lambda E: E.tensor_copy(T.m_row[:, :], T.pm[:, :]))
    ctx.op("vector", lambda E: E.tensor_copy(T.v_row[:, :], T.ps[:, :]))
    ctx.op("vector", lambda E: E.scalar_tensor_tensor(
        out=T.mr_row[:, :], in0=T.m_row[:, :], scalar=1.0, in1=T.m_row[:, :],
        op0=ALU.mult, op1=ALU.mult))
    ctx.op("vector", lambda E: E.scalar_tensor_tensor(
        out=T.v_row[:, :], in0=T.v_row[:, :], scalar=-1.0, in1=T.mr_row[:, :],
        op0=ALU.mult, op1=ALU.subtract))
    ctx.op("vector", lambda E: E.tensor_scalar(
        out=T.v_row[:, :], in0=T.v_row[:, :], scalar1=EPS, scalar2=None,
        op0=ALU.add), inc="dve")
    L["pm_free"] = ctx.val("dve")
    ctx.wait("scalar", "dve")
    ctx.op("scalar", lambda E: E.activation(
        out=T.mr_row[:, :], in_=T.v_row[:, :], func=AF.Ln))
    ctx.op("scalar", lambda E: E.activation(
        out=T.r_row[:, :], in_=T.mr_row[:, :], func=AF.Exp, scale=-0.5),
        inc="act")
    ctx.wait("vector", "act")
    ctx.op("vector", lambda E: E.tensor_tensor(
        T.mr_row[:, :], T.m_row[:, :], T.r_row[:, :], ALU.mult), inc="dve")
    ctx.wait("tensor", "dve")
    ctx.op("tensor", lambda E: E.matmul(
        T.pmisc[:, :], T.ones_row[:, :], T.r_row[:, :],
        start=True, stop=True), inc="pe")
    ctx.wait("vector", "pe")
    ctx.op("vector", lambda E: E.tensor_copy(T.bc_r[:, :], T.pmisc[:, :]),
           inc="dve")
    ctx.wait("tensor", "dve")
    ctx.op("tensor", lambda E: E.matmul(
        T.pmisc[:, :], T.ones_row[:, :], T.mr_row[:, :],
        start=True, stop=True), inc="pe")
    ctx.wait("vector", "pe")
    ctx.op("vector", lambda E: E.tensor_copy(T.bc_mr[:, :], T.pmisc[:, :]),
           inc="dve")
    L["bc_ready"] = ctx.val("dve")


def _rep_ap(t, n):
    return bass.AP(t, 0, [[TAU, 128], [0, n], [1, TAU]])


def emit_ln_norm(ctx, T):
    L = T.led
    gsp = getattr(T, "gp_norm", 8)
    bc_v = L["bc_ready"]
    ctx.wait("gpsimd", "dve", bc_v)
    ctx.wait("gpsimd", "pe", ctx.val("pe"))
    NB = 2
    for c0 in range(0, NCH, NB):
        gp = c0 >= NCH - gsp
        xnb = T.xn[:, c0 * TAU:(c0 + NB) * TAU].rearrange(
            "p (r f) -> p r f", r=NB)
        slb = T.slab[:, c0 * TAU:(c0 + NB) * TAU].rearrange(
            "p (r f) -> p r f", r=NB)
        if gp:
            ctx.op("gpsimd", lambda E, xnb=xnb, slb=slb: E.tensor_tensor(
                xnb, slb, _rep_ap(T.bc_r, NB), ALU.mult))
            ctx.op("gpsimd", lambda E, xnb=xnb: E.tensor_tensor(
                xnb, xnb, _rep_ap(T.bc_mr, NB), ALU.add), inc="gp")
            v = ("gp", ctx.val("gp"))
        else:
            btb = T.bt[:, 0:NB * TAU].rearrange("p (r f) -> p r f", r=NB)
            ctx.op("vector", lambda E, btb=btb, slb=slb: E.tensor_tensor(
                btb, slb, _rep_ap(T.bc_r, NB), ALU.mult))
            ctx.op("vector", lambda E, xnb=xnb, btb=btb: E.tensor_tensor(
                xnb, btb, _rep_ap(T.bc_mr, NB), ALU.add), inc="dve")
            v = ("dve", ctx.val("dve"))
        for c in range(c0, c0 + NB):
            L["xn_chunk"][c] = v
    L["xn_ready"] = ctx.val("dve")
    L["xn_ready_gp"] = ctx.val("gp")


def emit_gate_top2(ctx, T):
    L = T.led
    ctx.wait("tensor", "dve", max(L["bc_ready"], L["pglog_free"]))
    for i in range(NCH):
        ctx.op("tensor", lambda E, i=i: E.matmul(
            T.pglog[:, :], T.wgf_sb[:, i * 8:(i + 1) * 8],
            T.slab[:, i * TAU:(i + 1) * TAU],
            start=(i == 0), stop=(i == NCH - 1)), inc="pe")
    ctx.wait("vector", "pe")
    ctx.op("vector", lambda E: E.tensor_tensor(
        T.wfT[:, :], T.pglog[:, :], T.bc_r[0:8, :], ALU.mult))
    ctx.op("vector", lambda E: E.scalar_tensor_tensor(
        out=T.glog_sb[:, :], in0=T.bc_mr[0:8, :], scalar=T.gws_col[:, 0:1],
        in1=T.wfT[:, :], op0=ALU.mult, op1=ALU.add), inc="dve")
    L["pglog_free"] = ctx.val("dve")
    for tt in range(TAU // 128):
        ctx.wait("tensor", "dve")
        ctx.op("tensor", lambda E, tt=tt: E.transpose(
            T.ptr[:, 0:8], T.glog_sb[0:8, tt * 128:(tt + 1) * 128],
            T.ident_f[0:8, 0:8]), inc="pe")
        ctx.wait("vector", "pe")
        ctx.op("vector", lambda E, tt=tt: E.tensor_copy(
            T.logtm[:, tt * 8:(tt + 1) * 8], T.ptr[:, 0:8]), inc="dve")
    lg3 = T.logtm[:, :].rearrange("p (r f) -> p r f", r=4)
    eq3 = T.eq1[:, :].rearrange("p (r f) -> p r f", r=4)
    l23 = T.l2[:, :].rearrange("p (r f) -> p r f", r=4)
    wf3 = T.wf[:, :].rearrange("p (r f) -> p r f", r=4)

    def b3(m):
        return bass.AP(m, 0, [[4, 128], [1, 4], [0, 8]])
    for tt in range(TAU // 128):
        ctx.op("vector", lambda E, tt=tt: E.reduce_max(
            T.max1[:, tt:tt + 1], T.logtm[:, tt * 8:(tt + 1) * 8], axis=AX.X))
    ctx.op("vector", lambda E: E.tensor_tensor(
        eq3, lg3, b3(T.max1), ALU.is_equal))
    ctx.op("vector", lambda E: E.scalar_tensor_tensor(
        out=T.l2[:, :], in0=T.eq1[:, :], scalar=-1e30, in1=T.logtm[:, :],
        op0=ALU.mult, op1=ALU.add))
    for tt in range(TAU // 128):
        ctx.op("vector", lambda E, tt=tt: E.reduce_max(
            T.max2[:, tt:tt + 1], T.l2[:, tt * 8:(tt + 1) * 8], axis=AX.X))
    ctx.op("vector", lambda E: E.tensor_tensor(
        T.zrow[:, :], T.max2[:, :], T.max1[:, :], ALU.subtract), inc="dve")
    ctx.wait("scalar", "dve")
    ctx.op("scalar", lambda E: E.activation(
        out=T.e2[:, :], in_=T.zrow[:, :], func=AF.Exp, scale=1.0), inc="act")
    ctx.wait("vector", "act")
    ctx.op("vector", lambda E: E.tensor_scalar(
        out=T.zrow[:, :], in0=T.e2[:, :], scalar1=1.0, scalar2=None,
        op0=ALU.add))
    ctx.op("vector", lambda E: E.reciprocal(T.w1[:, :], T.zrow[:, :]))
    ctx.op("vector", lambda E: E.tensor_tensor(
        T.w2[:, :], T.e2[:, :], T.w1[:, :], ALU.mult))
    ctx.op("vector", lambda E: E.tensor_tensor(
        wf3, eq3, b3(T.w1), ALU.mult))
    ctx.op("vector", lambda E: E.tensor_tensor(
        eq3, l23, b3(T.max2), ALU.is_equal))
    ctx.op("vector", lambda E: E.tensor_tensor(
        l23, eq3, b3(T.w2), ALU.mult))
    ctx.op("vector", lambda E: E.tensor_tensor(
        T.wf[:, :], T.wf[:, :], T.l2[:, :], ALU.add), inc="dve")
    if getattr(T, "dump", None) is not None and T.cur_tau == 0:
        ctx.wait("gpsimd", "dve")
        ctx.dma("gpsimd", T.dump["wf"][:, :], T.wf[:, :], inc="gd")
        ctx.wait("vector", "gd", ctx.val("gd"))
        ctx.wait("tensor", "gd", ctx.val("gd"))
    for tt in range(TAU // 128):
        ctx.wait("tensor", "dve")
        ctx.op("tensor", lambda E, tt=tt: E.transpose(
            T.ptr[0:8, 0:128], T.wf[:, tt * 8:(tt + 1) * 8],
            T.ident_f[:, :]), inc="pe")
        ctx.wait("vector", "pe")
        ctx.op("vector", lambda E, tt=tt: E.tensor_copy(
            T.wfT[0:8, tt * 128:(tt + 1) * 128], T.ptr[0:8, 0:128]), inc="dve")
    ctx.wait("tensor", "dve")
    ctx.op("tensor", lambda E: E.matmul(
        T.pmisc[:, :], T.pat2[:, :], T.wfT[:, :],
        start=True, stop=True), inc="pe")
    if getattr(T, "dump", None) is not None and T.cur_tau == 0:
        ctx.wait("gpsimd", "dve")
        ctx.wait("gpsimd", "pe")
        ctx.dma("gpsimd", T.dump["glog"][:, :], T.glog_sb[:, :], inc="gd")
        ctx.dma("gpsimd", T.dump["wfT"][:, :], T.wfT[:, :], inc="gd")
        ctx.wait("vector", "gd", ctx.val("gd"))
        ctx.wait("tensor", "gd", ctx.val("gd"))


def wait_acc(ctx, eng, entry):
    if isinstance(entry, tuple):
        ctx.wait(eng, entry[0], entry[1])
    else:
        ctx.wait(eng, "dve", entry)


def emit_low_sl(ctx, T):
    L = T.led
    wait_acc(ctx, "tensor", L["acc_free"][0])
    for i in range(NCH):
        wait_xc(ctx, "tensor", L["xn_chunk"][i])
        ctx.op("tensor", lambda E, i=i: E.matmul(
            T.pacc[0][:, :], T.wa_sb[:, i * 128:(i + 1) * 128],
            T.xn[:, i * TAU:(i + 1) * TAU],
            start=(i == 0), stop=(i == NCH - 1)), inc="pe")
    ctx.wait("vector", "pe")
    ctx.op("vector", lambda E: E.tensor_copy(T.low_sb[:, :], T.pacc[0][:, :]))
    ctx.op("vector", lambda E: E.tensor_tensor(
        T.sl_bf[:, :], T.low_sb[:, :], T.pmisc[:, :], ALU.mult), inc="dve")
    L["sl_ready"] = ctx.val("dve")
    L["acc_free"][0] = ("dve", ctx.val("dve"))
    if getattr(T, "dump", None) is not None and T.cur_tau == 0:
        ctx.wait("gpsimd", "dve")
        ctx.dma("gpsimd", T.dump["low"][:, :], T.low_sb[:, :], inc="gd")
        ctx.dma("gpsimd", T.dump["sl"][:, :], T.sl_bf[:, :], inc="gd")


def emit_wtile(ctx, T, wd, pwd, j, tau, kind, ring, fast=False):
    L = T.led
    wbf = T.wbf[ring]
    if not CACHE_W:
        tau = 0
    if tau == 0:
        if kind == "sign":
            wl = T.wload[ring]
            H2 = D // 2
            hv = []
            rv = []
            wlds = []
            for hf in range(2):
                sl_ = slice(hf * H2, (hf + 1) * H2)
                lq = "scalar" if (fast and hf == 1) else "sync"
                wf_ = L["wload_free"][ring]
                if isinstance(wf_, tuple) and wf_[0] == "pe":
                    ctx.wait(lq, "pe", wf_[1])
                elif isinstance(wf_, tuple):
                    ctx.wait(lq, "act", wf_[0])
                    ctx.wait(lq, "dve", wf_[1])
                else:
                    ctx.wait(lq, "pe", wf_)
                sem = "wl2" if lq == "scalar" else "wld"
                ctx.dma(lq, wl[:, sl_], wd[j][:, sl_], inc=sem)
                wlds.append((sem, ctx.val(sem)))
            for hf in range(2):
                sl_ = slice(hf * H2, (hf + 1) * H2)
                ctx.wait("scalar", wlds[hf][0], wlds[hf][1])
                if hf == 0:
                    ctx.wait("scalar", "pe", L["wbf_free"][ring])
                    ctx.wait("scalar", "gd", L["wbf_gd"][ring])
                ctx.op("scalar", lambda E, wl=wl, wbf=wbf, sl_=sl_: E.activation(
                    out=wbf[:, sl_], in_=wl[:, sl_], func=AF.Sign), inc="act")
                hv.append(ctx.val("act"))
                ctx.wait("vector", wlds[hf][0], wlds[hf][1])
                ctx.op("vector", lambda E, wl=wl, sl_=sl_, hf=hf: E.reduce_sum(
                    T.sh2[:, hf:hf + 1], wl[:, sl_], axis=AX.X,
                    apply_absolute_value=True), inc="dve")
                rv.append(ctx.val("dve"))
            wbq = "gpsimd" if fast else "scalar"
            ctx.wait(wbq, "act", hv[1])
            ctx.dma(wbq, pwd[j], wbf[:, :], inc="gd")
            L["wbf_gd"][ring] = ctx.val("gd")
            if T.wload_single:
                L["wload_free"][0] = L["wload_free"][1] = ("pe", ctx.val("pe"))
            else:
                L["wload_free"][ring] = (hv[1], rv[1])
            scol = T.scol_cur
            ctx.op("vector", lambda E, scol=scol: E.tensor_tensor(
                scol, T.sh2[:, 0:1], T.sh2[:, 1:2], ALU.add))
            ctx.op("vector", lambda E, scol=scol: E.tensor_scalar(
                out=scol, in0=scol, scalar1=1.0 / D, scalar2=None,
                op0=ALU.mult), inc="dve")
            return ("act", hv[0], hv[1])
        else:
            ctx.wait("gpsimd", "pe", L["wbf_free"][ring])
            ctx.wait("gpsimd", "gd", L["wbf_gd"][ring])
            ctx.dma("gpsimd", wbf[:, :], wd[j], inc="gd")
            gd_v = ctx.val("gd")
            ctx.wait("sync", "gd", gd_v)
            ctx.dma("sync", pwd[j], wbf[:, :], inc="gd")
            L["wbf_gd"][ring] = ctx.val("gd")
            return ("gd", gd_v)
    else:
        ctx.wait("sync", "pe", L["wbf_free"][ring])
        ctx.wait("sync", "gd", L["prep_gd"])
        ctx.wait("sync", "gd", L["wbf_gd"][ring])
        ctx.dma("sync", wbf[:, :], pwd[j], inc="wld")
        return ("wld", ctx.val("wld"))


def emit_ffn_base(ctx, T, tau):
    L = T.led
    for j in range(NCH):
        ring = j % 2
        w_ready = emit_wtile(ctx, T, T.wb_d, T.pwb_d, j, tau, "cast", ring)
        wbf = T.wbf[ring]
        acc = T.pacc[j % 2]
        ctx.wait("tensor", w_ready[0], w_ready[1])
        wait_acc(ctx, "tensor", L["acc_free"][j % 2])
        for i in range(NCH):
            if j == 0:
                wait_xc(ctx, "tensor", L["xn_chunk"][i])
            ctx.op("tensor", lambda E, i=i, wbf=wbf, acc=acc: E.matmul(
                acc[:, :], wbf[:, i * 128:(i + 1) * 128],
                T.xn[:, i * TAU:(i + 1) * TAU],
                start=(i == 0), stop=False))
        if j == 0:
            ctx.wait("tensor", "dve", L["sl_ready"])
        ctx.op("tensor", lambda E, j=j, acc=acc: E.matmul(
            acc[:, :], T.wbp_sb[:, j * 128:(j + 1) * 128], T.sl_bf[:, :],
            start=False, stop=True), inc="pe")
        L["wbf_free"][ring] = ctx.val("pe")
        ctx.wait("vector", "pe")
        ev = T.ev_f[j % 2]
        ctx.op("vector", lambda E, j=j, acc=acc, ev=ev: E.tensor_scalar(
            out=ev[:, :], in0=acc[:, :], scalar1=0.5,
            scalar2=T.hbb_sb[:, j:j + 1], op0=ALU.mult, op1=ALU.add))
        ctx.op("vector", lambda E, j=j, ev=ev: E.tensor_tensor(
            T.slab[:, j * TAU:(j + 1) * TAU], T.slab[:, j * TAU:(j + 1) * TAU],
            ev[:, :], ALU.add), inc="dve")
        L["acc_free"][j % 2] = ("dve", ctx.val("dve"))
        L["x1_chunk"][j] = ctx.val("dve")
    L["x1_ready"] = ctx.val("dve")


def emit_qkv(ctx, T, tau):
    L = T.led
    pend = []

    def flush_pend(ctx):
        for fn in pend:
            fn(ctx)
        del pend[:]

    def defer_norm(ring, act_v, maxrow):
        def norm(ctx, ring=ring, act_v=act_v, maxrow=maxrow):
            sqb = T.sqb_f[ring]
            ctx.wait("tensor", "act", act_v)
            ctx.wait("tensor", "dve", L["pm_free"])
            ctx.op("tensor", lambda E, sqb=sqb: E.matmul(
                T.pm[:, :], T.ones_cb[:, :], sqb[:, :], start=True, stop=True),
                inc="pe")
            L["sqb_free"][ring] = ctx.val("pe")
            ctx.wait("vector", "pe")
            ctx.op("vector", lambda E: E.reduce_max(
                T.t11[:, :], T.pm[:, :], axis=AX.X))
            ctx.op("vector", lambda E, maxrow=maxrow: E.tensor_tensor(
                maxrow[:, :], maxrow[:, :], T.t11[:, :], ALU.max), inc="dve")
            L["pm_free"] = ctx.val("dve")
        pend.append(norm)

    for j in range(NKVC):
        ring = j % 2
        T.scol_cur = T.sk_cols[:, j:j + 1]
        w_ready = emit_wtile(ctx, T, T.wk_d, T.pwk_d, j, tau, "sign", ring, fast=True)
        wbf = T.wbf[ring]
        acc = T.pacc[j % 2]
        wait_acc(ctx, "tensor", L["acc_free"][j % 2])
        for i in range(NCH):
            if i == 0:
                ctx.wait("tensor", w_ready[0], w_ready[1])
            elif i == NCH // 2 and len(w_ready) > 2:
                ctx.wait("tensor", w_ready[0], w_ready[2])
            if j == 0:
                wait_xc(ctx, "tensor", L["xn_chunk"][i])
            ctx.op("tensor", lambda E, i=i, wbf=wbf, acc=acc: E.matmul(
                acc[:, :], wbf[:, i * 128:(i + 1) * 128],
                T.xn[:, i * TAU:(i + 1) * TAU],
                start=(i == 0), stop=(i == NCH - 1)), inc="pe")
        L["wbf_free"][ring] = ctx.val("pe")
        flush_pend(ctx)
        ctx.wait("vector", "pe")
        kqf = L["kq_free"][j % 2]
        if isinstance(kqf, tuple):
            ctx.wait("vector", kqf[0], kqf[1])
        else:
            ctx.wait("vector", "gd", kqf)
        kq = T.kq_bf[j % 2]
        ctx.op("vector", lambda E, acc=acc, kq=kq: E.tensor_copy(
            kq[:, :], acc[:, :]), inc="dve")
        dve_v = ctx.val("dve")
        ctx.wait("gpsimd", "dve", dve_v)
        ctx.dma("gpsimd", T.kin_d[tau][j * 128:(j + 1) * 128, :], kq[:, :],
                inc="kv")
        L["kq_free"][j % 2] = ("kv", ctx.val("kv"))
        ctx.wait("scalar", "dve", dve_v)
        ctx.wait("scalar", "pe", L["sqb_free"][j % 2])
        sqb = T.sqb_f[j % 2]
        ctx.op("scalar", lambda E, acc=acc, sqb=sqb: E.activation(
            out=sqb[:, :], in_=acc[:, :], func=AF.Square), inc="act")
        L["acc_free"][j % 2] = ("act", ctx.val("act"))
        defer_norm(j % 2, ctx.val("act"), T.maxk2)
    if tau == 0:
        ctx.wait("vector", "dve")
        for hk in range(NKVC):
            ctx.op("vector", lambda E, hk=hk: E.tensor_tensor(
                T.skv_cols[:, hk:hk + 1], T.sk_cols[:, hk:hk + 1],
                T.ks_sb[:, hk:hk + 1], ALU.mult), inc="dve")
    for j in range(NKVC):
        ring = j % 2
        T.scol_cur = T.sv_cols[:, j:j + 1]
        w_ready = emit_wtile(ctx, T, T.wv_d, T.pwv_d, j, tau, "sign", ring, fast=True)
        wbf = T.wbf[ring]
        acc = T.pacc[j % 2]
        if tau == 0:
            ctx.wait("vector", "dve")
            ctx.op("vector", lambda E, j=j: E.tensor_tensor(
                T.svp_cols[:, j:j + 1], T.sv_cols[:, j:j + 1],
                T.vs_sb[:, j:j + 1], ALU.mult), inc="dve")
        wait_acc(ctx, "tensor", L["acc_free"][j % 2])
        for i in range(NCH):
            if i == 0:
                ctx.wait("tensor", w_ready[0], w_ready[1])
            elif i == NCH // 2 and len(w_ready) > 2:
                ctx.wait("tensor", w_ready[0], w_ready[2])
            ctx.op("tensor", lambda E, i=i, wbf=wbf, acc=acc: E.matmul(
                acc[:, :], wbf[:, i * 128:(i + 1) * 128],
                T.xn[:, i * TAU:(i + 1) * TAU],
                start=(i == 0), stop=(i == NCH - 1)), inc="pe")
        L["wbf_free"][ring] = ctx.val("pe")
        flush_pend(ctx)
        vT = T.vT_bf[j % 2]
        ctx.wait("vector", "pe")
        ctx.wait("vector", "pe", L["vT_free"][j % 2])
        ctx.op("vector", lambda E, j=j, acc=acc, vT=vT: E.tensor_scalar(
            out=vT[:, :], in0=acc[:, :], scalar1=T.svp_cols[:, j:j + 1],
            scalar2=None, op0=ALU.mult), inc="dve")
        L["acc_free"][j % 2] = ("dve", ctx.val("dve"))

        def vtrans(ctx, j=j, vT=vT, dve_v=ctx.val("dve"), tau=tau):
            for tt in range(TAU // 128):
                pv = T.pvt[tt % 2]
                ctx.wait("tensor", "dve", dve_v)
                ctx.wait("tensor", "dve", L["pvt_free"][tt % 2])
                ctx.op("tensor", lambda E, tt=tt, vT=vT, pv=pv: E.transpose(
                    pv[:, :], vT[:, tt * 128:(tt + 1) * 128],
                    T.ident_b[:, :]), inc="pe")
                vtm = T.vtm_bf[tt % 2]
                ctx.wait("vector", "pe")
                ctx.wait("vector", "kv", L["vtm_free"][tt % 2])
                ctx.op("vector", lambda E, vtm=vtm, pv=pv: E.tensor_copy(
                    vtm[:, :], pv[:, :]), inc="dve")
                L["pvt_free"][tt % 2] = ctx.val("dve")
                ctx.wait("gpsimd", "dve")
                ctx.dma("gpsimd",
                        T.vin_d[tau][tt * 128:(tt + 1) * 128,
                                     j * 128:(j + 1) * 128],
                        vtm[:, :], inc="kv")
                L["vtm_free"][tt % 2] = ctx.val("kv")
            L["vT_free"][j % 2] = ctx.val("pe")
        pend.append(vtrans)
    flush_pend(ctx)
    ctx.wait("gpsimd", "kv")
    ctx.op("gpsimd", lambda E, tau=tau: E.collective_compute(
        "AllGather", ALU.bypass,
        replica_groups=[[0, 1], [2, 3], [4, 5], [6, 7]],
        ins=[T.kvin_d[tau].ap()],
        outs=[T.kvout_d[tau].ap()]), inc="cc")

    for j in range(NCH):
        ring = j % 2
        T.scol_cur = T.sq_cols[:, j:j + 1]
        w_ready = emit_wtile(ctx, T, T.wq_d, T.pwq_d, j, tau, "sign", ring, fast=True)
        wbf = T.wbf[ring]
        acc = T.pacc[j % 2]
        if tau == 0:
            ctx.wait("vector", "dve")
            ctx.op("vector", lambda E, j=j: E.tensor_tensor(
                T.effq_cols[:, j:j + 1], T.sq_cols[:, j:j + 1],
                T.qs_sb[:, j:j + 1], ALU.mult))
            ctx.op("vector", lambda E, j=j: E.scalar_tensor_tensor(
                out=T.effq_cols[:, j:j + 1], in0=T.skv_cols[:, j // 4:j // 4 + 1],
                scalar=INV_SQRT_HD, in1=T.effq_cols[:, j:j + 1],
                op0=ALU.mult, op1=ALU.mult), inc="dve")
        wait_acc(ctx, "tensor", L["acc_free"][j % 2])
        for i in range(NCH):
            if i == 0:
                ctx.wait("tensor", w_ready[0], w_ready[1])
            elif i == NCH // 2 and len(w_ready) > 2:
                ctx.wait("tensor", w_ready[0], w_ready[2])
            ctx.op("tensor", lambda E, i=i, wbf=wbf, acc=acc: E.matmul(
                acc[:, :], wbf[:, i * 128:(i + 1) * 128],
                T.xn[:, i * TAU:(i + 1) * TAU],
                start=(i == 0), stop=(i == NCH - 1)), inc="pe")
        L["wbf_free"][ring] = ctx.val("pe")
        flush_pend(ctx)
        ctx.wait("vector", "pe")
        kqf = L["kq_free"][j % 2]
        if isinstance(kqf, tuple):
            ctx.wait("vector", kqf[0], kqf[1])
        else:
            ctx.wait("vector", "gd", kqf)
        kq = T.kq_bf[j % 2]
        ctx.op("vector", lambda E, j=j, acc=acc, kq=kq: E.tensor_scalar(
            out=kq[:, :], in0=acc[:, :], scalar1=T.effq_cols[:, j:j + 1],
            scalar2=None, op0=ALU.mult), inc="dve")
        dve_v = ctx.val("dve")
        ctx.wait("gpsimd", "dve", dve_v)
        ctx.dma("gpsimd", T.qTs_d[j, :, tau * TAU:(tau + 1) * TAU], kq[:, :],
                inc="gd")
        L["kq_free"][j % 2] = ("gd", ctx.val("gd"))
        ctx.wait("scalar", "dve", dve_v)
        ctx.wait("scalar", "pe", L["sqb_free"][j % 2])
        sqb = T.sqb_f[j % 2]
        ctx.op("scalar", lambda E, j=j, acc=acc, sqb=sqb: E.activation(
            out=sqb[:, :], in_=acc[:, :], func=AF.Square,
            scale=T.effq_cols[:, j:j + 1]), inc="act")
        L["acc_free"][j % 2] = ("act", ctx.val("act"))
        defer_norm(j % 2, ctx.val("act"), T.maxq2)
    flush_pend(ctx)


def emit_p2(ctx, T):
    L = T.led
    negM_v = L["negM_v"]
    ctx.wait("sync", "gd")
    for tau in range(NTAU):
        ctx.wait("scalar", "cc", tau + 1)
        for hk in range(NKVC):
            for half in range(2):
                src = T.kvout_d[tau][half * 2 * KVD + hk * 128:
                                     half * 2 * KVD + hk * 128 + 128, :]
                dst = T.karena[:, hk * SEQ + half * TOK + tau * TAU:
                               hk * SEQ + half * TOK + tau * TAU + TAU]
                ctx.dma("scalar", dst, src, inc="ka")
        for hk in range(NKVC):
            for kc4 in range(8):
                half, r4 = kc4 // 4, kc4 % 4
                kc = half * 16 + tau * 4 + r4
                vv = T.kvout_d[tau][half * 2 * KVD + KVD:
                                    half * 2 * KVD + 2 * KVD, :].rearrange(
                    "(a b) t -> a (b t)", b=2)
                src = vv[r4 * 128: r4 * 128 + 128, hk * 128:(hk + 1) * 128]
                dst = T.varena[:, hk * SEQ + kc * 128:
                               hk * SEQ + kc * 128 + 128]
                ctx.dma("scalar", dst, src, inc="ka")
            L["ka_thk"][tau][hk] = ctx.val("ka")
            if tau == NTAU - 1:
                L["ka_hk"][hk] = ctx.val("ka")

    ctx.wait("scalar", "dve", negM_v)
    npair = NKC // 2
    kcseq = [half * 16 + tau * 4 + r
             for tau in range(NTAU) for half in range(2) for r in range(4)]

    def qk_pair(h, hk, qq, qr, p):
        st = T.pstp[p % 2]
        ctx.wait("tensor", "act", L["stp_free"][p % 2])
        if p % 4 == 0:
            ctx.wait("tensor", "ka", L["ka_thk"][p // 4][hk])
        for half in range(2):
            kc = kcseq[2 * p + half]
            ctx.op("tensor", lambda E, hk=hk, qq=qq, kc=kc, st=st, qr=qr,
                   half=half: E.matmul(
                       st[:, half * TAU:(half + 1) * TAU],
                       T.karena[:, hk * SEQ + kc * 128:
                                hk * SEQ + kc * 128 + 128],
                       qr[:, qq * TAU:(qq + 1) * TAU],
                       start=True, stop=True),
                   inc=("pe" if half == 1 else None))
        L["qkp_pe"][p % 2] = ctx.val("pe")

    def exp_pair(p):
        st = T.pstp[p % 2]
        pt = T.pt_bf[p % 2]
        ctx.wait("scalar", "pe",
                 max(L["qkp_pe"][p % 2], L["ptp_free"][p % 2]))
        ctx.op("scalar", lambda E, st=st, pt=pt: E.activation(
            out=pt[:, :], in_=st[:, :], func=AF.Exp,
            bias=T.negM[:, 0:1], scale=1.0), inc="act", chain=False)
        L["stp_free"][p % 2] = ctx.val("act")

    def av_pair(h, hk, qq, ot, p):
        pt = T.pt_bf[p % 2]
        ctx.wait("tensor", "act", L["stp_free"][p % 2])
        if p == 0:
            oidx = (h * NTAU + qq) % 2
            ctx.wait("tensor", "dve",
                     max(L["ot_free"][oidx], L["sums_copy"]))
        for half in range(2):
            kc = kcseq[2 * p + half]
            first = (p == 0 and half == 0)
            last = (p == npair - 1 and half == 1)
            sl_ = pt[:, half * TAU:(half + 1) * TAU]
            ctx.op("tensor", lambda E, hk=hk, kc=kc, ot=ot, sl_=sl_,
                   first=first, last=last: E.matmul(
                ot[:, :],
                T.varena[:, hk * SEQ + kc * 128: hk * SEQ + kc * 128 + 128],
                sl_, start=first, stop=last))
            ctx.op("tensor", lambda E, kc=kc, sl_=sl_, first=first, last=last:
                   E.matmul(
                T.psums0[:, :], T.ones_cb[:, :], sl_,
                start=first, stop=last),
                inc=("pe" if half == 1 else None))
        L["ptp_free"][p % 2] = ctx.val("pe")

    pending = []

    def flush_tail(ctx):
        for fn in pending:
            fn(ctx)
        del pending[:]

    def make_tail(h, qq, oidx, ot, recip_v, rec_buf):
        def tail(ctx):
            ctx.wait("tensor", "dve", recip_v)
            ctx.wait("tensor", "dve", L["prb_free"])
            ctx.op("tensor", lambda E: E.matmul(
                T.prb[:, :], T.ones_row[:, :], rec_buf[:, :],
                start=True, stop=True), inc="pe")
            ctx.wait("vector", "pe")
            ctx.op("vector", lambda E: E.tensor_copy(T.rb_sb[:, :],
                                                     T.prb[:, :]), inc="dve")
            L["prb_free"] = ctx.val("dve")
            obf = T.obf[oidx]
            ctx.wait("vector", "gd", L["obf_free"][oidx])
            ctx.op("vector", lambda E, ot=ot, obf=obf: E.tensor_tensor(
                obf[:, :], ot[:, :], T.rb_sb[:, :], ALU.mult), inc="dve")
            L["ot_free"][oidx] = ctx.val("dve")
            ctx.wait("gpsimd", "dve")
            ctx.dma("gpsimd", T.oTs_d[h, :, qq * TAU:(qq + 1) * TAU],
                    obf[:, :], inc="gd")
            L["obf_free"][oidx] = ctx.val("gd")
        return tail

    for h in range(NH):
        hk = h // 4
        qr = T.qh_bf[h % 2]
        ctx.wait("sync", "pe", L["qh_free"][h % 2])
        ctx.dma("sync", qr[:, :], T.qTs_d[h], inc="qld")
        qld_v = ctx.val("qld")
        ctx.wait("tensor", "qld", qld_v)
        for qq in range(NTAU):
            oidx = (h * NTAU + qq) % 2
            ot = T.pot[oidx]
            for p in range(npair):
                if p == 0:
                    qk_pair(h, hk, qq, qr, 0)
                    if npair > 1:
                        qk_pair(h, hk, qq, qr, 1)
                elif p + 1 < npair:
                    qk_pair(h, hk, qq, qr, p + 1)
                if p == 3:
                    flush_tail(ctx)
                exp_pair(p)
                av_pair(h, hk, qq, ot, p)
            pe_tail = ctx.val("pe")
            srow = T.sums_sb[oidx]
            rec_buf = T.rec_rows[oidx]
            ctx.wait("vector", "pe", pe_tail)
            ctx.op("vector", lambda E, srow=srow: E.tensor_copy(
                srow[:, :], T.psums0[:, :]), inc="dve")
            L["sums_copy"] = ctx.val("dve")
            ctx.op("vector", lambda E, srow=srow, rec_buf=rec_buf:
                   E.reciprocal(rec_buf[:, :], srow[:, :]), inc="dve")
            pending.append(make_tail(h, qq, oidx, ot, ctx.val("dve"), rec_buf))
        L["qh_free"][h % 2] = ctx.val("pe")
    flush_tail(ctx)


def emit_oproj(ctx, T, tau):
    L = T.led
    for j in range(NCH):
        ring = j % 2
        w_ready = emit_wtile(ctx, T, T.wo_d, T.pwo_d, j, NTAU + tau, "sign",
                             ring)
        wbf = T.wbf[ring]
        acc = T.pacc[j % 2]
        ctx.wait("tensor", w_ready[0], w_ready[1])
        wait_acc(ctx, "tensor", L["acc_free"][j % 2])
        for i in range(NCH):
            if j == 0:
                ctx.wait("tensor", "oslab", L["os_piece"][i // 8])
            ctx.op("tensor", lambda E, i=i, wbf=wbf, acc=acc: E.matmul(
                acc[:, :], wbf[:, i * 128:(i + 1) * 128],
                T.oslab[:, i * TAU:(i + 1) * TAU],
                start=(i == 0), stop=(i == NCH - 1)), inc="pe")
        L["wbf_free"][ring] = ctx.val("pe")
        ctx.wait("vector", "pe")
        ctx.wait("vector", "xld", L["x1_piece"][j // 8])
        ev = T.ev_f[j % 2]
        ctx.op("vector", lambda E, j=j, acc=acc, ev=ev: E.tensor_scalar(
            out=ev[:, :], in0=acc[:, :], scalar1=T.effo_cols[:, j:j + 1],
            scalar2=None, op0=ALU.mult))
        ctx.op("vector", lambda E, j=j, ev=ev: E.tensor_tensor(
            T.slab[:, j * TAU:(j + 1) * TAU], T.slab[:, j * TAU:(j + 1) * TAU],
            ev[:, :], ALU.add), inc="dve")
        L["acc_free"][j % 2] = ("dve", ctx.val("dve"))
        L["x1_chunk"][j] = ctx.val("dve")



def _walk(ctx, T, stage):
    nc = ctx.nc
    nm = lambda s: f"{s}_{ctx.active}"

    ctx.dma("sync", T.ident_f[:, :], T.idf_d[:, :], inc="xld")
    ctx.dma("sync", T.ident_b[:, :], T.idb_d[:, :], inc="xld")
    ctx.dma("sync", T.ones_row[:, :], T.onesrow_d[:, :], inc="xld")
    ctx.dma("sync", T.invd[:, :], T.invd_d[:, :], inc="xld")
    ctx.dma("sync", T.invdb[:, :], T.invdb_d[:, :], inc="xld")
    ctx.dma("sync", T.ones_cb[:, :], T.onescb_d[:, :], inc="xld")
    ctx.dma("sync", T.pat2[:, :], T.pat2_d[:, :], inc="xld")
    ctx.dma("gpsimd", T.wa_sb[:, :], T.wa_d[:, :], inc="gd")
    ctx.dma("gpsimd", T.wbp_sb[:, :], T.wbp_d[:, :], inc="gd")
    ctx.dma("sync", T.wgf_sb[:, :], T.wg_d[:, :], inc="xld")
    ctx.dma("sync", T.ones_cf[:, :], T.onescf_d[:, :], inc="xld")
    ctx.dma("sync", T.qs_sb[:, :], T.qs_d[:, :], inc="xld")
    ctx.dma("sync", T.ks_sb[:, :], T.ks_d[:, :], inc="xld")
    ctx.dma("sync", T.vs_sb[:, :], T.vs_d[:, :], inc="xld")
    ctx.dma("sync", T.osc_sb[:, :], T.os_d[:, :], inc="xld")
    ctx.dma("sync", T.hbb_sb[:, :], T.bb_d[:, :], inc="xld")
    setup_xld = ctx.val("xld")
    setup_gd = ctx.val("gd")
    ctx.wait("vector", "xld", setup_xld)
    ctx.wait("vector", "gd", setup_gd)
    ctx.op("vector", lambda E: E.tensor_scalar(
        out=T.hbb_sb[:, :], in0=T.hbb_sb[:, :], scalar1=0.5, scalar2=None,
        op0=ALU.mult))
    ctx.op("vector", lambda E: E.memset(T.maxq2[:, :], 0.0))
    ctx.op("vector", lambda E: E.memset(T.maxk2[:, :], 0.0), inc="dve")
    setup_dve = ctx.val("dve")
    for e in ("tensor", "scalar", "gpsimd", "sync"):
        ctx.wait(e, "dve", setup_dve)
        ctx.wait(e, "xld", setup_xld)

    T.led = {
        "sq_free": [0, 0], "sqb_free": [0, 0], "acc_free": [0, 0],
        "wload_free": [0, 0], "wbf_free": [0, 0], "wcast_free": 0,
        "ptr_free": 0, "pvt_free": [0, 0], "kq_free": [0, 0],
        "wbf_gd": [0, 0],
        "vtm_free": [0, 0], "vT_free": [0, 0], "xn_ready": 0,
        "sl_ready": 0,
        "xn_chunk": [0] * NCH, "bc_ready": 0, "pglog_free": 0,
        "xb_free": [0, 0],
        "x1_chunk": [0] * NCH, "x1_piece": [0] * 4, "negM_v": 0,
        "x1_ready": 0, "cc_done": 0, "pm_free": 0, "prep_gd": 0,
        "slab_free": 0, "slab_free_gd": 0,
    }
    L = T.led
    T.wload_single = False

    with contextlib.ExitStack() as stk:
        sb = lambda n_, sh, dt: stk.enter_context(
            nc.sbuf_tensor(nm(n_), sh, dt))
        psb = lambda n_, sh, dt=F32: stk.enter_context(
            nc.psum_tensor(nm(n_), sh, dt))
        T.slab = sb("slab", [128, NCH * TAU], F32)
        T.xn = sb("xn", [128, NCH * TAU], BF16)
        T.wload = [sb("wload0", [128, D], F32), sb("wload1", [128, D], F32)]
        T.wbf = [sb("wbf0", [128, D], BF16), sb("wbf1", [128, D], BF16)]
        T.sh2 = sb("sh2", [128, 2], F32)
        T.sqb_f = [sb("sqb0", [128, TAU], BF16), sb("sqb1", [128, TAU], BF16)]
        T.xbf_f = [sb("xbf0", [128, TAU], BF16), sb("xbf1", [128, TAU], BF16)]
        T.bc_r = sb("bc_r", [128, TAU], F32)
        T.bc_mr = sb("bc_mr", [128, TAU], F32)
        tt1 = sb("ttf0", [128, TAU], F32)
        T.tt_f = [tt1, tt1]
        T.ev_f = T.tt_f
        T.bt = sb("bt", [128, 2 * TAU], F32)
        T.gp_norm = 8
        T.kq_bf = [sb("kq0", [128, TAU], BF16), sb("kq1", [128, TAU], BF16)]
        T.vT_bf = [sb("vT_bf0", [128, TAU], BF16),
                   sb("vT_bf1", [128, TAU], BF16)]
        T.vtm_bf = [sb("vtm0", [128, 128], BF16), sb("vtm1", [128, 128], BF16)]
        T.low_sb = sb("low_sb", [128, TAU], F32)
        T.sl_bf = sb("sl_bf", [128, TAU], BF16)
        T.glog_sb = sb("glog_sb", [8, TAU], F32)
        T.logtm = sb("logtm", [128, 32], F32)
        T.wfT = sb("wfT", [8, TAU], F32)
        T.max1 = sb("max1", [128, 4], F32)
        T.max2 = sb("max2", [128, 4], F32)
        T.e2 = sb("e2c", [128, 4], F32)
        T.w1 = sb("w1c", [128, 4], F32)
        T.w2 = sb("w2c", [128, 4], F32)
        T.zrow = sb("zrow", [128, 4], F32)
        T.eq1 = sb("eq1", [128, 32], F32)
        T.l2 = sb("l2t", [128, 32], F32)
        T.wf = sb("wft", [128, 32], F32)
        T.m_row = sb("m_row", [1, TAU], F32)
        T.v_row = sb("v_row", [1, TAU], F32)
        T.r_row = sb("r_row", [1, TAU], F32)
        T.mr_row = sb("mr_row", [1, TAU], F32)
        T.pacc = [psb("pacc0", [128, TAU]), psb("pacc1", [128, TAU])]
        T.pm = psb("pm", [1, TAU])
        T.ps = psb("ps", [1, TAU])
        T.pmisc = psb("pmisc", [128, TAU])
        T.pglog = T.pmisc[0:8, :]
        T.ptr = psb("ptr", [128, 128])
        T.pvt = [psb("pvt0", [128, 128], BF16),
                 psb("pvt1", [128, 128], BF16)]

        for i in range(NCH):
            ctx.op("tensor", lambda E, i=i: E.matmul(
                T.pglog[:, 0:1], T.wgf_sb[:, i * 8:(i + 1) * 8],
                T.ones_cf[:, :], start=(i == 0), stop=(i == NCH - 1)),
                inc="pe")
        ctx.wait("vector", "pe")
        ctx.op("vector", lambda E: E.tensor_copy(
            T.gws_col[:, :], T.pglog[0:8, 0:1]), inc="dve")

        for tau in range(NTAU):
            T.cur_tau = tau
            ctx.wait("scalar", "dve", L["slab_free"])
            ctx.wait("scalar", "gd", L["slab_free_gd"])
            for pc in range(4):
                ctx.dma("scalar",
                        T.slab[:, pc * 8 * TAU:(pc + 1) * 8 * TAU].rearrange(
                            "p (c t) -> p c t", c=8),
                        T.xT_d[pc * 8:(pc + 1) * 8, :,
                               tau * TAU:(tau + 1) * TAU].rearrange(
                            "c p t -> p c t"), inc="xld")
                L["x1_piece"][pc] = ctx.val("xld")
            emit_ln_stats(ctx, T, lambda c: ("xld", L["x1_piece"][c // 8]))
            emit_gate_top2(ctx, T)
            emit_ln_norm(ctx, T)
            emit_low_sl(ctx, T)
            emit_ffn_base(ctx, T, tau)
            ctx.wait("gpsimd", "dve", L["x1_ready"])
            ctx.dma("gpsimd",
                    T.x1s_d[:, :, tau * TAU:(tau + 1) * TAU].rearrange(
                        "c p t -> p c t"),
                    T.slab[:, :].rearrange("p (c t) -> p c t", c=NCH),
                    inc="gd")
            x1dma_v = ctx.val("gd")
            if stage == "ffn1":
                L["slab_free"] = L["x1_ready"]
                L["slab_free_gd"] = x1dma_v
                L["prep_gd"] = ctx.val("gd")
                continue
            emit_ln_stats(ctx, T, lambda c: ("dve", L["x1_chunk"][c]))
            if tau > 0:
                lo = [0, 0, 11, 22][tau]
                hi = [0, 11, 22, NCH][tau]
                for j in range(lo, hi):
                    ring = j % 2
                    T.scol_cur = T.so_cols[:, j:j + 1]
                    emit_wtile(ctx, T, T.wo_d, T.pwo_d, j, 0, "sign", ring)
                    L["wbf_free"][ring] = ctx.val("pe")
                    ctx.op("vector", lambda E, j=j: E.tensor_tensor(
                        T.effo_cols[:, j:j + 1], T.so_cols[:, j:j + 1],
                        T.osc_sb[:, j:j + 1], ALU.mult))
            emit_ln_norm(ctx, T)
            emit_qkv(ctx, T, tau)
            L["slab_free"] = ctx.val("dve")
            L["slab_free_gd"] = x1dma_v
            L["prep_gd"] = ctx.val("gd")
        L["cc_done"] = ctx.val("cc")
        if stage == "full":
            ctx.op("vector", lambda E: E.tensor_tensor(
                T.t11[:, :], T.maxq2[:, :], T.maxk2[:, :], ALU.mult),
                inc="dve")
            ctx.wait("scalar", "dve")
            ctx.op("scalar", lambda E: E.activation(
                out=T.t11b[:, :], in_=T.t11[:, :], func=AF.Ln))
            ctx.op("scalar", lambda E: E.activation(
                out=T.t11[:, :], in_=T.t11b[:, :], func=AF.Exp, scale=0.5),
                inc="act")
            ctx.wait("vector", "act")
            ctx.op("vector", lambda E: E.tensor_scalar(
                out=T.t11[:, :], in0=T.t11[:, :], scalar1=-1.0, scalar2=None,
                op0=ALU.mult), inc="dve")
            ctx.wait("tensor", "dve")
            ctx.wait("tensor", "dve", L["ptr_free"])
            ctx.op("tensor", lambda E: E.matmul(
                T.ptr[:, 0:1], T.ones_row[:, :], T.t11[:, :],
                start=True, stop=True), inc="pe")
            ctx.wait("vector", "pe")
            ctx.op("vector", lambda E: E.tensor_copy(
                T.negM[:, :], T.ptr[:, 0:1]), inc="dve")
            L["negM_v"] = ctx.val("dve")

    if stage in ("ffn1", "p1"):
        ctx.barrier()
        ctx.wait("gpsimd", "gd")
        if T.kdbg_d is not None:
            ctx.wait("gpsimd", "cc", ctx.val("cc"))
            for t in range(NTAU):
                for half in range(2):
                    ctx.dma("gpsimd",
                            T.kdbg_d[t][half * KVD:(half + 1) * KVD, :],
                            T.kvout_d[t][half * 2 * KVD:
                                         half * 2 * KVD + KVD, :], inc="out")
                    vv = T.kvout_d[t][half * 2 * KVD + KVD:
                                      half * 2 * KVD + 2 * KVD, :].rearrange(
                        "(a b) x -> a (b x)", b=2)
                    ctx.dma("gpsimd",
                            T.vdbg_d[t][half * TAU:(half + 1) * TAU, :],
                            vv, inc="out")
        ctx.dma("gpsimd", T.yT_d[:, :, :], T.x1s_d[:, :, :], inc="out")
        ctx.wait("gpsimd", "out", ctx.val("out"))
        return

    ctx.barrier()

    with contextlib.ExitStack() as stk:
        sb = lambda n_, sh, dt: stk.enter_context(
            nc.sbuf_tensor(nm(n_), sh, dt))
        psb = lambda n_, sh, dt=F32: stk.enter_context(
            nc.psum_tensor(nm(n_), sh, dt))
        T.karena = sb("karena", [128, NKVC * SEQ], BF16)
        T.varena = sb("varena", [128, NKVC * SEQ], BF16)
        T.qh_bf = [sb("qh0", [128, TOK], BF16), sb("qh1", [128, TOK], BF16)]
        T.pt_bf = [sb(f"ptp{i}", [128, 2 * TAU], BF16) for i in range(2)]
        T.rb_sb = sb("rb_sb", [128, TAU], F32)
        T.obf = [sb("obf0", [128, TAU], BF16), sb("obf1", [128, TAU], BF16)]
        T.sums_sb = [sb("sums0", [1, TAU], F32), sb("sums1", [1, TAU], F32)]
        T.rec_rows = [sb("rec_row0", [1, TAU], F32),
                      sb("rec_row1", [1, TAU], F32)]
        T.pstp = [stk.enter_context(nc.psum_tensor(
            nm(f"pstp{i}"), [128, 2 * TAU], F32, unsafe=True)) for i in range(2)]
        T.pot = [psb(f"pot{i}", [128, TAU]) for i in range(2)]
        T.psums0 = psb("psums0", [1, TAU])
        T.prb = psb("prb", [128, TAU])
        L.update({
            "stp_free": [0, 0], "ptp_free": [0, 0], "qkp_pe": [0, 0],
            "ot_free": [0, 0], "sums_copy": 0, "obf_free": [0, 0],
            "qh_free": [0, 0], "prb_free": 0, "ka_hk": [0] * NKVC,
            "ka_thk": [[0] * NKVC for _ in range(NTAU)],
        })
        emit_p2(ctx, T)

    ctx.barrier()

    if stage == "p2":
        ctx.wait("gpsimd", "gd")
        ctx.dma("gpsimd", T.yT_d[:, :, :], T.oTs_d[:, :, :], inc="out")
        ctx.wait("gpsimd", "out", ctx.val("out"))
        return

    with contextlib.ExitStack() as stk:
        sb = lambda n_, sh, dt: stk.enter_context(
            nc.sbuf_tensor(nm(n_), sh, dt))
        psb = lambda n_, sh, dt=F32: stk.enter_context(
            nc.psum_tensor(nm(n_), sh, dt))
        T.slab = sb("slab3", [128, NCH * TAU], F32)
        T.oslab = sb("oslab", [128, NCH * TAU], BF16)
        T.xn = sb("xn3", [128, NCH * TAU], BF16)
        T.wload = [None, None]
        T.wload_single = True
        T.wbf = [sb("wbf30", [128, D], BF16), sb("wbf31", [128, D], BF16)]
        T.wcast = None
        T.sqb_f = [sb("sqb30", [128, TAU], BF16),
                   sb("sqb31", [128, TAU], BF16)]
        T.xbf_f = [sb("xbf30", [128, TAU], BF16), sb("xbf31", [128, TAU], BF16)]
        T.bc_r = sb("bc_r3", [128, TAU], F32)
        T.bc_mr = sb("bc_mr3", [128, TAU], F32)
        tt3 = sb("ttf30", [128, TAU], F32)
        T.tt_f = [tt3, tt3]
        T.ev_f = T.tt_f
        T.bt = sb("bt3", [128, 2 * TAU], F32)
        T.gp_norm = 8
        T.low_sb = sb("low_sb3", [128, TAU], F32)
        T.sl_bf = sb("sl_bf3", [128, TAU], BF16)
        T.glog_sb = sb("glog3", [8, TAU], F32)
        T.logtm = sb("logtm3", [128, 32], F32)
        T.wfT = sb("wfT3", [8, TAU], F32)
        T.max1 = sb("max13", [128, 4], F32)
        T.max2 = sb("max23", [128, 4], F32)
        T.e2 = sb("e2c3", [128, 4], F32)
        T.w1 = sb("w1c3", [128, 4], F32)
        T.w2 = sb("w2c3", [128, 4], F32)
        T.zrow = sb("zrow3", [128, 4], F32)
        T.eq1 = sb("eq13", [128, 32], F32)
        T.l2 = sb("l2t3", [128, 32], F32)
        T.wf = sb("wft3", [128, 32], F32)
        T.m_row = sb("m_row3", [1, TAU], F32)
        T.v_row = sb("v_row3", [1, TAU], F32)
        T.r_row = sb("r_row3", [1, TAU], F32)
        T.mr_row = sb("mr_row3", [1, TAU], F32)
        T.pacc = [psb("pacc30", [128, TAU]), psb("pacc31", [128, TAU])]
        T.pm = psb("pm3", [1, TAU])
        T.ps = psb("ps3", [1, TAU])
        T.pmisc = psb("pmisc3", [128, TAU])
        T.pglog = T.pmisc[0:8, :]
        T.ptr = psb("ptr3", [128, 128])
        L.update({
            "sq_free": [0, 0], "acc_free": [0, 0], "wload_free": [0, 0],
            "wbf_free": [0, 0], "wcast_free": 0, "ptr_free": 0,
            "slab_free": 0, "slab_free_gd": 0, "oslab_free": 0,
            "wbf_gd": [0, 0], "os_piece": [0] * 4,
            "oslab_ready": 0, "out_prev": 0,
        })

        ctx.wait("scalar", "gd")
        for tau in range(NTAU):
            ctx.wait("scalar", "dve", L["slab_free"])
            ctx.wait("scalar", "out", L["out_prev"])
            for pc in range(4):
                ctx.dma("scalar",
                        T.slab[:, pc * 8 * TAU:(pc + 1) * 8 * TAU].rearrange(
                            "p (c t) -> p c t", c=8),
                        T.x1s_d[pc * 8:(pc + 1) * 8, :,
                                tau * TAU:(tau + 1) * TAU].rearrange(
                            "c p t -> p c t"), inc="xld")
                L["x1_piece"][pc] = ctx.val("xld")
            x1_v = ctx.val("xld")
            ctx.wait("scalar", "pe", L["oslab_free"])
            for pc in range(4):
                ctx.dma("scalar",
                        T.oslab[:, pc * 8 * TAU:(pc + 1) * 8 * TAU].rearrange(
                            "p (c t) -> p c t", c=8),
                        T.oTs_d[pc * 8:(pc + 1) * 8, :,
                                tau * TAU:(tau + 1) * TAU].rearrange(
                            "c p t -> p c t"), inc="oslab")
                L["os_piece"][pc] = ctx.val("oslab")
            L["oslab_ready"] = ctx.val("oslab")
            emit_oproj(ctx, T, tau)
            L["oslab_free"] = ctx.val("pe")
            emit_ln_stats(ctx, T, lambda c: ("dve", L["x1_chunk"][c]))
            emit_gate_top2(ctx, T)
            emit_ln_norm(ctx, T)
            emit_low_sl(ctx, T)
            emit_ffn_base(ctx, T, NTAU + tau)
            if tau == 0:
                L["prep_gd"] = ctx.val("gd")
            ctx.wait("gpsimd", "dve", L["x1_ready"])
            ctx.dma("gpsimd",
                    T.yT_d[:, :, tau * TAU:(tau + 1) * TAU].rearrange(
                        "c p t -> p c t"),
                    T.slab[:, :].rearrange("p (c t) -> p c t", c=NCH),
                    inc="out")
            L["slab_free"] = ctx.val("dve")
            L["out_prev"] = ctx.val("out")
        ctx.wait("gpsimd", "out", ctx.val("out"))
        ctx.wait("sync", "out", ctx.val("out"))


def build_nc(stage=None, dbg=None):
    stage = STAGE if stage is None else stage
    dbg = DBG if dbg is None else dbg
    nc = bass.Bass("TRN2", target_bir_lowering=False)
    T = Tset = type("Tset", (), {})()

    def din(name, shape, dt=F32):
        return nc.dram_tensor(name, shape, dt, kind="ExternalInput")

    def dsc(name, shape, dt):
        if dbg:
            return nc.dram_tensor(name, shape, dt, kind="ExternalOutput")
        return nc.dram_tensor(name, shape, dt)

    T.xT_d = din("xT", [NCH, 128, TOK])
    T.wq_d = din("wq", [NCH, 128, D])
    T.wk_d = din("wk", [NKVC, 128, D])
    T.wv_d = din("wv", [NKVC, 128, D])
    T.wo_d = din("wo", [NCH, 128, D])
    T.wb_d = din("wb", [NCH, 128, D])
    T.wa_d = din("wa", [128, D])
    T.wbp_d = din("wbp", [128, D])
    T.wg_d = din("wg", [128, NCH * 8])
    T.onescf_d = din("onescf", [128, 1])
    T.qs_d = din("qs", [128, NCH])
    T.ks_d = din("ks", [128, NKVC])
    T.vs_d = din("vs", [128, NKVC])
    T.os_d = din("osc", [128, NCH])
    T.bb_d = din("bb", [128, NCH])
    T.idf_d = din("idf", [128, 128])
    T.idb_d = din("idb", [128, 128], BF16)
    T.onesrow_d = din("onesrow", [1, 128])
    T.invd_d = din("invd", [128, 1])
    T.invdb_d = din("invdb", [128, 1], BF16)
    T.onescb_d = din("onescb", [128, 1], BF16)
    T.pat2_d = din("pat2", [8, 128])

    T.pwq_d = nc.dram_tensor("pwq", [NCH, 128, D], BF16)
    T.pwk_d = nc.dram_tensor("pwk", [NKVC, 128, D], BF16)
    T.pwv_d = nc.dram_tensor("pwv", [NKVC, 128, D], BF16)
    T.pwo_d = nc.dram_tensor("pwo", [NCH, 128, D], BF16)
    T.pwb_d = nc.dram_tensor("pwb", [NCH, 128, D], BF16)
    T.x1s_d = dsc("x1s", [NCH, 128, TOK], F32)
    T.qTs_d = dsc("qTs", [NH, 128, TOK], BF16)
    T.oTs_d = dsc("oTs", [NH, 128, TOK], BF16)
    T.kvin_d = [nc.dram_tensor(f"kvin{t}", [2 * KVD, TAU], BF16)
                for t in range(NTAU)]
    T.kin_d = [kv[0:KVD, :] for kv in T.kvin_d]
    T.vin_d = [kv[KVD:2 * KVD, :].rearrange("(a b) t -> a (b t)", b=2)
               for kv in T.kvin_d]
    T.kvout_d = [nc.dram_tensor(f"kvout{t}", [4 * KVD, TAU], BF16)
                 for t in range(NTAU)]
    T.kvo_k = [kv for kv in T.kvout_d]
    T.kvo_v = [kv for kv in T.kvout_d]
    T.kdbg_d = T.vdbg_d = None
    if dbg:
        T.kdbg_d = [nc.dram_tensor(f"kdbg{t}", [2 * KVD, TAU], BF16,
                                   kind="ExternalOutput") for t in range(NTAU)]
        T.vdbg_d = [nc.dram_tensor(f"vdbg{t}", [2 * TAU, KVD], BF16,
                                   kind="ExternalOutput") for t in range(NTAU)]
    T.yT_d = nc.dram_tensor("yT", [NCH, 128, TOK], F32, kind="ExternalOutput")
    T.dump = None
    T.cur_tau = -1
    if dbg:
        T.dump = {
            "glog": nc.dram_tensor("dglog", [8, TAU], F32, kind="ExternalOutput"),
            "wfT": nc.dram_tensor("dwfT", [8, TAU], F32, kind="ExternalOutput"),
            "low": nc.dram_tensor("dlow", [128, TAU], F32, kind="ExternalOutput"),
            "sl": nc.dram_tensor("dsl", [128, TAU], BF16, kind="ExternalOutput"),
            "logtm": nc.dram_tensor("dlogtm", [128, 8], F32, kind="ExternalOutput"),
            "wf": nc.dram_tensor("dwf", [128, 8], F32, kind="ExternalOutput"),
            "max1": nc.dram_tensor("dmax1", [128, 1], F32, kind="ExternalOutput"),
            "eq1": nc.dram_tensor("deq1", [128, 8], F32, kind="ExternalOutput"),
            "w1": nc.dram_tensor("dw1", [128, 1], F32, kind="ExternalOutput"),
        }

    sem_names = ["gd", "wld", "wl2", "xld", "act", "dve", "pe", "cc", "ka",
                 "qld", "oslab", "out", "bar", "kv", "gp"]

    with contextlib.ExitStack() as st:
        T.ident_f = st.enter_context(nc.sbuf_tensor("ident_f", [128, 128], F32))
        T.ident_b = st.enter_context(nc.sbuf_tensor("ident_b", [128, 128], BF16))
        T.ones_row = st.enter_context(nc.sbuf_tensor("ones_row", [1, 128], F32))
        T.invd = st.enter_context(nc.sbuf_tensor("invd_s", [128, 1], F32))
        T.invdb = st.enter_context(nc.sbuf_tensor("invdb_s", [128, 1], BF16))
        T.ones_cb = st.enter_context(nc.sbuf_tensor("ones_cb", [128, 1], BF16))
        T.pat2 = st.enter_context(nc.sbuf_tensor("pat2_s", [8, 128], F32))
        T.wa_sb = st.enter_context(nc.sbuf_tensor("wa_sb", [128, D], BF16))
        T.wbp_sb = st.enter_context(nc.sbuf_tensor("wbp_sb", [128, D], BF16))
        T.wgf_sb = st.enter_context(nc.sbuf_tensor("wgf_sb", [128, NCH * 8], F32))
        T.gws_col = st.enter_context(nc.sbuf_tensor("gws_col", [8, 1], F32))
        T.ones_cf = st.enter_context(nc.sbuf_tensor("ones_cf", [128, 1], F32))
        T.qs_sb = st.enter_context(nc.sbuf_tensor("qs_sb", [128, NCH], F32))
        T.ks_sb = st.enter_context(nc.sbuf_tensor("ks_sb", [128, NKVC], F32))
        T.vs_sb = st.enter_context(nc.sbuf_tensor("vs_sb", [128, NKVC], F32))
        T.osc_sb = st.enter_context(nc.sbuf_tensor("osc_sb", [128, NCH], F32))
        T.hbb_sb = st.enter_context(nc.sbuf_tensor("hbb_sb", [128, NCH], F32))
        T.sq_cols = st.enter_context(nc.sbuf_tensor("sq_cols", [128, NCH], F32))
        T.sk_cols = st.enter_context(nc.sbuf_tensor("sk_cols", [128, NKVC], F32))
        T.sv_cols = st.enter_context(nc.sbuf_tensor("sv_cols", [128, NKVC], F32))
        T.so_cols = st.enter_context(nc.sbuf_tensor("so_cols", [128, NCH], F32))
        T.skv_cols = st.enter_context(nc.sbuf_tensor("skv_cols", [128, NKVC], F32))
        T.svp_cols = st.enter_context(nc.sbuf_tensor("svp_cols", [128, NKVC], F32))
        T.effq_cols = st.enter_context(nc.sbuf_tensor("effq_cols", [128, NCH], F32))
        T.effo_cols = st.enter_context(nc.sbuf_tensor("effo_cols", [128, NCH], F32))
        T.maxq2 = st.enter_context(nc.sbuf_tensor("maxq2", [1, 1], F32))
        T.maxk2 = st.enter_context(nc.sbuf_tensor("maxk2", [1, 1], F32))
        T.t11 = st.enter_context(nc.sbuf_tensor("t11", [1, 1], F32))
        T.t11b = st.enter_context(nc.sbuf_tensor("t11b", [1, 1], F32))
        T.negM = st.enter_context(nc.sbuf_tensor("negM", [128, 1], F32))
        T.junk = st.enter_context(nc.sbuf_tensor("junk", [128, 128], F32))
        sems = {s: _Sem(st.enter_context(nc.semaphore(s))) for s in sem_names}

        with nc.Block() as blk:
            for eng in ENGINES:
                def section(E, eng=eng):
                    ctx = Ctx(nc, eng, {k: _Sem(v.h) for k, v in sems.items()})
                    _walk(ctx, T, stage)
                getattr(blk, eng)(section)
    return nc, T



def _wtile(W):
    ofn = W.shape[0] // 128
    t = W.T.reshape(NCH, 128, ofn, 128).transpose(2, 1, 0, 3)
    return np.ascontiguousarray(t.reshape(ofn, 128, D))


def prep_inputs(inputs):
    x = np.asarray(inputs["L0_attn_hidden_states"], dtype=np.float32)
    wq = _wtile(np.asarray(inputs["L0_attn_q_weight"], np.float32))
    wk = _wtile(np.asarray(inputs["L0_attn_k_weight"], np.float32))
    wv = _wtile(np.asarray(inputs["L0_attn_v_weight"], np.float32))
    wo = _wtile(np.asarray(inputs["L0_attn_o_weight"], np.float32))
    wb = _wtile(np.asarray(inputs["L0_ffn_base_w"], np.float32))
    la = np.asarray(inputs["L0_ffn_lora_A"], np.float32).reshape(128, D)
    wa = np.ascontiguousarray(
        la.T.reshape(NCH, 128, 128).transpose(1, 0, 2).reshape(128, D))
    wbp = np.ascontiguousarray(
        np.asarray(inputs["L0_ffn_lora_B"], np.float32).transpose(0, 2, 1)
        .reshape(128, D))
    wg = np.ascontiguousarray(
        np.asarray(inputs["L0_ffn_gate_w"], np.float32).T
        .reshape(NCH, 128, 8).transpose(1, 0, 2).reshape(128, NCH * 8))

    def cols(v, n):
        return np.ascontiguousarray(np.asarray(v, np.float32).reshape(n, 128).T)

    qs = cols(inputs["L0_attn_q_scale"], NCH)
    ks = cols(inputs["L0_attn_k_scale"], NKVC)
    vs = cols(inputs["L0_attn_v_scale"], NKVC)
    osc = cols(inputs["L0_attn_o_scale"], NCH)
    bb = cols(inputs["L0_ffn_base_b"], NCH)

    import ml_dtypes
    idf = np.eye(128, dtype=np.float32)
    idb = np.eye(128, dtype=ml_dtypes.bfloat16)
    onesrow = np.ones((1, 128), np.float32)
    invd = np.full((128, 1), -1.0 / D, np.float32)
    invdb = np.full((128, 1), -1.0 / D, ml_dtypes.bfloat16)
    onescb = np.ones((128, 1), ml_dtypes.bfloat16)
    pat2 = np.zeros((8, 128), np.float32)
    for e in range(8):
        pat2[e, e * 16:(e + 1) * 16] = 2.0

    shared = dict(wq=wq, wk=wk, wv=wv, wo=wo, wb=wb, wa=wa, wbp=wbp, wg=wg,
                  qs=qs, ks=ks, vs=vs, osc=osc, bb=bb, idf=idf, idb=idb,
                  onesrow=onesrow, invd=invd, invdb=invdb, onescb=onescb,
                  onescf=np.ones((128, 1), np.float32), pat2=pat2)
    in_maps = []
    for c in range(8):
        b, half = c // 2, c % 2
        xc = x[b, half * TOK:(half + 1) * TOK, :]
        xT = np.ascontiguousarray(xc.T).reshape(NCH, 128, TOK)
        m = dict(shared)
        m["xT"] = xT
        in_maps.append(m)
    return in_maps


def _enable_ldw_opt():
    try:
        from concourse.compiler_utils import (get_compiler_flags,
                                              set_compiler_flags)
        flags = [f.replace("--enable-ldw-opt=false", "--enable-ldw-opt=true")
                 for f in get_compiler_flags()]
        set_compiler_flags(flags)
    except Exception:
        pass


def run_cores(inputs, stage=None, dbg=None, **kw):
    _enable_ldw_opt()
    in_maps = prep_inputs(inputs)
    nc, _ = build_nc(stage=stage, dbg=dbg)
    return run_bass_kernel_spmd(nc, in_maps, core_ids=list(range(8)), **kw)


def kernel(**inputs):
    res = run_cores(inputs)
    y = np.empty((4, SEQ, D), np.float32)
    for c in range(8):
        b, half = c // 2, c % 2
        yT = np.asarray(res.results[c]["yT"], np.float32).reshape(D, TOK)
        y[b, half * TOK:(half + 1) * TOK, :] = yT.T
    return y
